# version 03 (final)
# baseline (speedup 1.0000x reference)
import numpy as np
import ml_dtypes
from contextlib import ExitStack

import concourse.bass as bass
import concourse.mybir as mybir
from concourse.bass_utils import run_bass_kernel_spmd
from concourse.alu_op_type import AluOpType as OP

F32 = mybir.dt.float32
BF16 = mybir.dt.bfloat16
I32 = mybir.dt.int32
U8 = mybir.dt.uint8
AF = mybir.ActivationFunctionType
AX = mybir.AxisListType
IOA = bass.IndirectOffsetOnAxis
BF = ml_dtypes.bfloat16

T = 2048
D = 1024
NDT = D // 128
NTT = T // 128
BB = 2048

ENGINES = ("sync", "tensor", "vector", "scalar", "gpsimd")

CONFIG = {"groups": [[0], [1], [2], [3]]}


class Chain:
    def __init__(self, prog, name):
        self.prog = prog
        self.name = name
        self.n = 0
        self.ops = []

    def wait(self, other, count):
        if count <= 0:
            return
        sem = other.semname if isinstance(other, Chain) else other
        def f(eng, sem=sem, count=count):
            eng.wait_ge(self.prog.sems[sem], count)
        self.ops.append(f)

    def op(self, method, *args, self_wait=True, inc=True, **kwargs):
        prev = self.n
        semname = self.semname
        if inc:
            self.n += 1
        def f(eng, method=method, args=args, kwargs=kwargs, prev=prev,
              self_wait=self_wait, inc=inc):
            if self_wait and prev > 0:
                eng.wait_ge(self.prog.sems[semname], prev)
            inst = getattr(eng, method)(*args, **kwargs)
            if inc:
                inst.then_inc(self.prog.sems[semname], 1)
        self.ops.append(f)
        return self.n

    @property
    def semname(self):
        return "ch_" + self.name


class PB:
    def __init__(self, ap, name):
        self.ap = ap
        self.name = name
        self.last_consumer = None
        self.ready = 0

    def __getitem__(self, idx):
        return self.ap[idx]


class Prog:
    def __init__(self, nc):
        self.nc = nc
        self.pbs = []
        self.es = ExitStack()
        self.chains = {e: Chain(self, e) for e in ENGINES}
        self.Tc = self.chains["tensor"]
        self.V = self.chains["vector"]
        self.S = self.chains["scalar"]
        self.G = self.chains["gpsimd"]
        self.SY = self.chains["sync"]
        self.sems = {}
        self.dma_counts = {"sync": 0, "gpsimd": 0}
        self._uid = 0

    def uid(self):
        self._uid += 1
        return self._uid

    def sbuf(self, name, shape, dt, es=None):
        return (es or self.es).enter_context(
            self.nc.sbuf_tensor(f"{name}_{self.uid()}", shape, dt))

    def psum(self, name, shape, dt, es=None):
        ap = (es or self.es).enter_context(
            self.nc.psum_tensor(f"{name}_{self.uid()}", shape, dt))
        pb = PB(ap, name)
        self.pbs.append(pb)
        return pb

    def pb_alias(self, ap, name):
        pb = PB(ap, name)
        self.pbs.append(pb)
        return pb

    def dma(self, q, out, in_, **kw):
        ch = self.chains[q]
        self.dma_counts[q] += 16
        tgt = self.dma_counts[q]
        semname = "dma_" + q
        def f(eng, out=out, in_=in_, kw=kw):
            eng.dma_start(out=out, in_=in_, **kw).then_inc(self.sems[semname], 16)
        ch.ops.append(f)
        return tgt

    def dma_wait(self, chain, q, tgt):
        chain.wait("dma_" + q, tgt)

    def barrier(self):
        for pb in self.pbs:
            pb.last_consumer = None
        counts = {e: self.chains[e].n for e in ENGINES}
        dmas = dict(self.dma_counts)
        for e in ENGINES:
            ch = self.chains[e]
            for o in ENGINES:
                if o != e and counts[o] > 0:
                    ch.wait(self.chains[o], counts[o])
            for q in ("sync", "gpsimd"):
                if dmas[q] > 0:
                    ch.wait("dma_" + q, dmas[q])

    def acquire(self, *pbs):
        for b in pbs:
            if b.last_consumer is not None:
                ch, cnt = b.last_consumer
                self.Tc.wait(ch, cnt)
                b.last_consumer = None

    def tw(self):
        for ch in (self.V, self.S, self.G):
            if ch.n > 0:
                self.Tc.wait(ch, ch.n)

    def mm(self, out_ap, lhsT, rhs, start, stop, inc=False):
        return self.Tc.op("matmul", out_ap, lhsT, rhs, start=start, stop=stop,
                          self_wait=False, inc=inc)

    def build(self):
        for e in ENGINES:
            self.sems["ch_" + e] = self.es.enter_context(
                self.nc.semaphore("ch_" + e))
        for q in ("sync", "gpsimd"):
            self.sems["dma_" + q] = self.es.enter_context(
                self.nc.semaphore("dma_" + q))
        with self.nc.Block() as block:
            for ename in ENGINES:
                section = getattr(block, ename)
                chain = self.chains[ename]
                def make(chain):
                    def body(eng):
                        for f in chain.ops:
                            f(eng)
                    return body
                section(make(chain))
        self.es.close()


def ceil_div(a, b):
    return (a + b - 1) // b


class FM:
    def __init__(self, P, name, rows, dt, tokens=T, es=None):
        self.rows = rows
        self.p = min(rows, 128)
        self.nt = ceil_div(rows, 128)
        self.tokens = tokens
        self.dt = dt
        self.sb = P.sbuf(name, [self.p, self.nt * tokens], dt, es=es)

    def tile(self, i):
        return self.sb[:, i * self.tokens:(i + 1) * self.tokens]

    def chunk(self, i, c0, w):
        base = i * self.tokens + c0
        return self.sb[:, base:base + w]



def stage_guard(P, C, key, q="sync"):
    prev = C["stage_consumer"].get(key)
    if prev is not None:
        ch, cnt = prev
        P.chains[q].wait(ch, cnt)


def load_weight_bf16(P, C, dram_ap, rows, cols, name, conv="gpsimd", es=None,
                     dst=None, guard=None):
    nrt = ceil_div(rows, 128)
    p = min(rows, 128)
    wb = dst or FM(P, "w_" + name, rows, BF16, tokens=cols, es=es)
    ch = P.chains[conv]
    if guard is not None:
        gch, gcnt = guard
        ch.wait(gch, gcnt)
    stg = C["wstage"]
    sc = stg.tokens
    i = 0
    for r in range(nrt):
        pr = min(128, rows - r * 128)
        for c0 in range(0, cols, sc):
            cw = min(sc, cols - c0)
            key = ("wstage", 0)
            stage_guard(P, C, key)
            buf = stg.tile(0)[0:pr, 0:cw]
            tgt = P.dma("sync", buf, dram_ap[r * 128: r * 128 + pr, c0:c0 + cw])
            P.dma_wait(ch, "sync", tgt)
            cnt = ch.op("tensor_copy", wb.tile(r)[0:pr, c0:c0 + cw], buf)
            C["stage_consumer"][key] = (ch, cnt)
            i += 1
    return wb


def evac_simple(P, C, pb, dst_ap, bias_col=None, act=None, eng="vector"):
    ch = P.chains[eng]
    ch.wait(P.Tc, pb.ready)
    p = dst_ap.shape[0]
    w = dst_ap.shape[-1]
    if act is not None:
        kw = {}
        if bias_col is not None:
            kw["bias"] = bias_col
        cnt = ch.op("activation", dst_ap, pb.ap[0:p, 0:w], act, **kw)
    elif bias_col is not None:
        cnt = ch.op("tensor_scalar", dst_ap, pb.ap[0:p, 0:w], bias_col, None,
                    OP.add)
    else:
        cnt = ch.op("tensor_copy", dst_ap, pb.ap[0:p, 0:w])
    pb.last_consumer = (ch, cnt)
    return cnt


def load_row_bf16(P, C, dram_ap, n, dst_bf_ap):
    V = P.V
    for c0 in range(0, n, 2048):
        w = min(2048, n - c0)
        key = ("rowstage",)
        stage_guard(P, C, key)
        t = P.dma("sync", C["row_a"].sb[0:1, 0:w], dram_ap[0:1, c0:c0 + w])
        P.dma_wait(V, "sync", t)
        cc = V.op("tensor_copy", dst_bf_ap[0:1, c0:c0 + w],
                  C["row_a"].sb[0:1, 0:w])
        C["stage_consumer"][key] = (V, cc)
    return V.n


def emit_proj_fm(P, C, wfm, x_fm, out_rows, epilogue, tokens=T, ocol_off=0,
                 ktiles=None):
    nko = wfm.nt if ktiles is None else ktiles
    not_ = ceil_div(out_rows, 128)
    pa = C["ps_a"]
    P.tw()
    for o in range(not_):
        ow = min(128, out_rows - o * 128)
        for c0 in range(0, tokens, BB):
            w = min(BB, tokens - c0)
            P.acquire(pa)
            for k in range(nko):
                lhsT = wfm.tile(k)[:, ocol_off + o * 128: ocol_off + o * 128 + ow]
                for j in range(0, w, 512):
                    jw = min(512, w - j)
                    rhs = x_fm.chunk(k, c0 + j, jw)
                    last = (k == nko - 1 and j + 512 >= w)
                    P.mm(pa[0:ow, j:j + jw], lhsT, rhs, k == 0, k == nko - 1,
                         inc=last)
            pa.ready = P.Tc.n
            epilogue(o, c0, w, pa)


def bcast_row(P, C, row_ap, dst_sb, width, f32row=False, pbs=None):
    pbs = pbs or (C["ps1"], C["ps2"])
    ones = C["cst"]["ones_row_f32" if f32row else "ones_row_bf16"]
    V = P.V
    P.tw()
    for idx, j in enumerate(range(0, width, 512)):
        jw = min(512, width - j)
        pb = pbs[idx % len(pbs)]
        P.acquire(pb)
        P.mm(pb[0:128, 0:jw], ones, row_ap[0:1, j:j + jw], True, True, inc=True)
        pb.ready = P.Tc.n
        V.wait(P.Tc, pb.ready)
        c = V.op("tensor_copy", dst_sb[:, j:j + jw], pb[0:128, 0:jw])
        pb.last_consumer = (V, c)
    return V.n


def emit_layernorm(P, C, src_fm, dst_fm, gain_col=None, bias_col=None,
                   src_dram=None):
    V, S, Tc = P.V, P.S, P.Tc
    pa = C["ps_a"]
    onesb = C["cst"]["col_inv1024"]
    sq = C["scr_sq"]
    b1 = C["scr_b1"]
    b2 = C["scr_b2"]
    r_mean, r_m2, r_rstd = C["row_a"], C["row_b"], C["row_c"]
    if src_dram is not None:
        for i in range(NDT):
            key = ("x", 0)
            stage_guard(P, C, key)
            tgt = P.dma("sync", C["xstage"].tile(0),
                        src_dram[i * 128:(i + 1) * 128, :])
            P.dma_wait(V, "sync", tgt)
            cv = V.op("tensor_copy", dst_fm.tile(i), C["xstage"].tile(0))
            C["stage_consumer"][key] = (V, cv)
        xsrc = dst_fm
    else:
        xsrc = src_fm
    P.acquire(pa)
    sq_consumed = {0: 0, 1: 0}
    for i in range(NDT):
        V.wait(Tc, sq_consumed[i % 2])
        cq = V.op("tensor_mul", sq.chunk(i % 2, 0, T), xsrc.tile(i),
                  xsrc.tile(i))
        Tc.wait(V, cq)
        for j in range(0, T, 512):
            P.mm(pa[0:1, j:j + 512], onesb, xsrc.chunk(i, j, 512),
                 i == 0, i == NDT - 1)
            P.mm(pa[32:33, j:j + 512], onesb, sq.chunk(i % 2, j, 512),
                 i == 0, i == NDT - 1,
                 inc=(j == T - 512))
        sq_consumed[i % 2] = Tc.n
    pa.ready = Tc.n
    V.wait(Tc, pa.ready)
    c1 = V.op("tensor_copy", r_mean.sb[0:1, 0:T], pa[0:1, 0:T])
    c2 = V.op("tensor_copy", r_m2.sb[0:1, 0:T], pa[32:33, 0:T])
    pa.last_consumer = (V, c2)
    V.op("tensor_mul", r_rstd.sb[0:1, 0:T], r_mean.sb[0:1, 0:T],
         r_mean.sb[0:1, 0:T])
    c3 = V.op("tensor_tensor", r_rstd.sb[0:1, 0:T], r_m2.sb[0:1, 0:T],
              r_rstd.sb[0:1, 0:T], OP.subtract)
    S.wait(V, c3)
    c4 = S.op("activation", r_rstd.sb[0:1, 0:T], r_rstd.sb[0:1, 0:T], AF.Sqrt,
              bias=C["cst"]["c_eps_ln"][0:1, 0:1])
    V.wait(S, c4)
    V.op("reciprocal", r_rstd.sb[0:1, 0:T], r_rstd.sb[0:1, 0:T])
    V.op("tensor_mul", r_mean.sb[0:1, 0:T], r_mean.sb[0:1, 0:T],
         r_rstd.sb[0:1, 0:T])
    V.op("tensor_scalar", r_mean.sb[0:1, 0:T], r_mean.sb[0:1, 0:T], -1.0,
         None, OP.mult)
    bcast_row(P, C, r_rstd.sb, b1.sb, T, f32row=True)
    bcast_row(P, C, r_mean.sb, b2.sb, T, f32row=True)
    for i in range(NDT):
        xb = xsrc.tile(i)
        V.op("tensor_mul", dst_fm.tile(i), xb, b1.sb[:, 0:T])
        V.op("tensor_add", dst_fm.tile(i), dst_fm.tile(i), b2.sb[:, 0:T])
        if gain_col is not None:
            V.op("tensor_scalar", dst_fm.tile(i), dst_fm.tile(i),
                 gain_col(i), bias_col(i), OP.mult, OP.add)
    return V.n



def emit_mamba(P, C, li, xn, out_fm):
    V, S, Tc = P.V, P.S, P.Tc
    W = C["W"]
    pfx = f"L{li}a_"
    es = ExitStack()
    w1 = load_weight_bf16(P, C, W[pfx + "w1"], D, D, pfx + "w1", es=es)
    w23 = load_weight_bf16(P, C, W[pfx + "w23"], D, 32, pfx + "w23", es=es)
    b1col = C["colbuf"](P, C, W[pfx + "b1c"], 8, pfx + "b1c", es)
    b23col = C["colbuf"](P, C, W[pfx + "b23c"], 2, pfx + "b23c", es, p=16)
    bc = FM(P, "bc16", 16, BF16, tokens=2 * T, es=es)
    t1 = C["scr_sq"]

    for half in range(2):
        def ep_bc(o, c0, w, pb, half=half):
            V.wait(Tc, pb.ready)
            c = V.op("tensor_scalar", bc.chunk(0, half * T + c0, w)[0:16, :],
                     pb[0:16, 0:w], b23col[0:16, half:half + 1], None, OP.add)
            pb.last_consumer = (V, c)
        emit_proj_fm(P, C, w23, xn, 16, ep_bc, ocol_off=half * 16)

    pv = V.op("tensor_mul", t1.sb[0:16, 0:T], bc.sb[0:16, 0:T],
              bc.sb[0:16, T:2 * T])
    pd = C["ps1"]
    P.acquire(pd)
    Tc.wait(V, pv)
    dotrow = FM(P, "row_bf", 1, BF16, es=es)
    for j in range(0, T, 512):
        P.mm(pd[0:1, 0:512], C["cst"]["col_ones16"][0:16, 0:1],
             t1.sb[0:16, j:j + 512], True, True, inc=True)
        pd.ready = P.Tc.n
        V.wait(Tc, pd.ready)
        c = V.op("tensor_copy", dotrow.sb[0:1, j:j + 512], pd[0:1, 0:512])
        pd.last_consumer = (V, c)
        P.acquire(pd)
    dotb = C["scr_b1"]
    bcast_row(P, C, dotrow.sb, dotb.sb, T)

    def ep_delta(o, c0, w, pb):
        bcol = b1col[:, o:o + 1]
        V.wait(Tc, pb.ready)
        V.wait(S, S.n)
        V.op("tensor_scalar", t1.chunk(0, 0, w), pb[:, 0:w], bcol, None,
             OP.add)
        a2 = V.op("tensor_scalar", t1.chunk(1, 0, w), pb[:, 0:w], bcol, 0.0,
                  OP.add, OP.max)
        pb.last_consumer = (V, a2)
        a1 = V.op("scalar_tensor_tensor", t1.chunk(0, 0, w), t1.chunk(1, 0, w),
                  -2.0, t1.chunk(0, 0, w), OP.mult, OP.add)
        S.wait(V, a1)
        S.op("activation", t1.chunk(0, 0, w), t1.chunk(0, 0, w), AF.Exp)
        a3 = S.op("activation", t1.chunk(0, 0, w), t1.chunk(0, 0, w), AF.Ln,
                  bias=C["cst"]["c_one"][0:128, 0:1])
        V.wait(S, a3)
        V.op("tensor_add", t1.chunk(0, 0, w), t1.chunk(0, 0, w),
             t1.chunk(1, 0, w))
        if CONFIG.get("mamba_mode") == "u":
            cu = V.op("tensor_scalar", out_fm.chunk(o, c0, w), pb[:, 0:w],
                      bcol, None, OP.add)
            pb.last_consumer = (V, cu)
        elif CONFIG.get("mamba_mode") == "delta":
            V.op("tensor_copy", out_fm.chunk(o, c0, w), t1.chunk(0, 0, w))
        elif CONFIG.get("mamba_mode") == "dot":
            V.op("tensor_copy", out_fm.chunk(o, c0, w), dotb.sb[:, c0:c0 + w])
        else:
            V.op("tensor_mul", t1.chunk(0, 0, w), t1.chunk(0, 0, w),
                 xn.chunk(o, c0, w))
            V.op("tensor_mul", out_fm.chunk(o, c0, w), t1.chunk(0, 0, w),
                 dotb.sb[:, c0:c0 + w])

    emit_proj_fm(P, C, w1, xn, D, ep_delta)
    P.barrier()
    es.close()


class FMView:
    def __init__(self, sb, rows, tokens):
        self.sb = sb
        self.rows = rows
        self.p = min(rows, 128)
        self.nt = ceil_div(rows, 128)
        self.tokens = tokens

    def tile(self, i):
        return self.sb[:, i * self.tokens:(i + 1) * self.tokens]

    def chunk(self, i, c0, w):
        base = i * self.tokens + c0
        return self.sb[:, base:base + w]


def phi_evac(P, C, pb, bcol, dst_ap, w):
    V, S = P.V, P.S
    t1 = C["scr_sq"]
    V.wait(P.Tc, pb.ready)
    V.wait(S, S.n)
    a1 = V.op("tensor_scalar", t1.chunk(0, 0, w), pb[:, 0:w], bcol, 0.0,
              OP.add, OP.min)
    a2 = V.op("tensor_scalar", t1.chunk(1, 0, w), pb[:, 0:w], bcol, 0.0,
              OP.add, OP.max)
    pb.last_consumer = (V, a2)
    S.wait(V, a1)
    a3 = S.op("activation", t1.chunk(0, 0, w), t1.chunk(0, 0, w), AF.Exp)
    V.wait(S, a3)
    return V.op("tensor_add", dst_ap, t1.chunk(0, 0, w), t1.chunk(1, 0, w))



def emit_l0_attn(P, C, li, xn, out_unused, x_in_dram, x_out_dram):
    V, S, Tc, G, SY = P.V, P.S, P.Tc, P.G, P.chains["sync"]
    W = C["W"]
    DI = 2048
    es = ExitStack()
    pa = C["ps_a"]
    singles = [C["ps1"], C["ps2"], C["ps3"], C["ps4"]]
    bqc = colbuf(P, C, W["L0a_bqc"], 16, "bqc", es)
    bkc = colbuf(P, C, W["L0a_bkc"], 16, "bkc", es)
    boc = colbuf(P, C, W["L0a_boc"], 8, "boc", es)
    bv_es = ExitStack()
    bvrow_b = P.sbuf("l0bvb", [1, DI], BF16, es=bv_es)
    load_row_bf16(P, C, W["L0a_bv"], DI, bvrow_b[0:1, 0:DI])
    SY.wait(S, S.n)
    SY.wait(V, V.n)
    for i in range(NDT):
        t = P.dma("sync", C["g_spill"][i * 128:(i + 1) * 128, :],
                  C["g"].tile(i))
    gs_done = P.dma_counts["sync"]

    def ktile(o):
        return (C["g"].tile(o) if o < NDT else C["s"].tile(o - NDT))

    wslot = FM(P, "l0wslot", D, BF16, tokens=1024, es=bv_es)
    wconsumed = [0]

    def load_half(dram, half):
        load_weight_bf16(P, C, dram[:, half * 1024:(half + 1) * 1024], D, 1024,
                         "l0w", dst=wslot, guard=(Tc, wconsumed[0]))

    kev = 0
    for half in range(2):
        load_half(W["L0a_wk"], half)
        if half == 0:
            V.wait("dma_sync", gs_done)
        for o8 in range(8):
            o = half * 8 + o8
            P.tw()
            P.acquire(pa)
            for k in range(NDT):
                lhsT = wslot.tile(k)[:, o8 * 128:(o8 + 1) * 128]
                for j in range(0, T, 512):
                    P.mm(pa[:, j:j + 512], lhsT, xn.chunk(k, j, 512),
                         k == 0, k == NDT - 1,
                         inc=(k == NDT - 1 and j == T - 512))
            pa.ready = Tc.n
            kev = phi_evac(P, C, pa, bkc[:, o:o + 1], ktile(o), T)
        wconsumed[0] = Tc.n
    vstage = C["scr_b1"]
    vst_tgt = 0
    for half in range(2):
        load_half(W["L0a_wv"], half)
        for tt in range(NTT):
            P.tw()
            P.acquire(pa)
            for k in range(NDT):
                lhsT = xn.tile(k)[:, tt * 128:(tt + 1) * 128]
                for j in range(0, 1024, 512):
                    P.mm(pa[:, j:j + 512], lhsT, wslot.chunk(k, j, 512),
                         k == 0, False)
            for j in range(0, 1024, 512):
                P.mm(pa[0:128, j:j + 512], C["cst"]["ones_row_bf16"],
                     bvrow_b[0:1, half * 1024 + j: half * 1024 + j + 512],
                     False, True, inc=(j == 512))
            pa.ready = Tc.n
            V.wait(Tc, pa.ready)
            if vst_tgt:
                P.dma_wait(V, "sync", vst_tgt)
            cv = V.op("tensor_copy", vstage.sb[:, 0:1024], pa[:, 0:1024])
            pa.last_consumer = (V, cv)
            SY.wait(V, cv)
            vst_tgt = P.dma("sync",
                            C["v_spill"][tt, :, half * 1024:(half + 1) * 1024],
                            vstage.sb[:, 0:1024])
        wconsumed[0] = Tc.n
    qstage = C["scr_b2"]
    qst_tgt = 0
    for half in range(2):
        load_half(W["L0a_wq"], half)
        for o8 in range(8):
            o = half * 8 + o8
            P.tw()
            P.acquire(pa)
            for k in range(NDT):
                lhsT = wslot.tile(k)[:, o8 * 128:(o8 + 1) * 128]
                for j in range(0, T, 512):
                    P.mm(pa[:, j:j + 512], lhsT, xn.chunk(k, j, 512),
                         k == 0, k == NDT - 1,
                         inc=(k == NDT - 1 and j == T - 512))
            pa.ready = Tc.n
            if qst_tgt:
                P.dma_wait(V, "sync", qst_tgt)
            cq = phi_evac(P, C, pa, bqc[:, o:o + 1], qstage.sb[:, 0:T], T)
            SY.wait(V, cq)
            qst_tgt = P.dma("sync", C["q_spill"][o], qstage.sb[:, 0:T])
        wconsumed[0] = Tc.n
    P.barrier()
    bv_es.close()
    woX = FMView(xn.sb, DI, 1024)
    load_weight_bf16(P, C, W["L0a_wo"], DI, 1024, "l0wo", dst=woX)
    qbuf = P.sbuf("l0qbuf", [128, 16 * 512], BF16, es=es)
    scM = P.sbuf("l0scm", [128, 16 * 512], BF16, es=es)
    vbuf = P.sbuf("l0vbuf", [128, 2 * 512], BF16, es=es)
    gbufc = P.sbuf("l0gc", [128, 2 * 512], BF16, es=es)
    rb = P.sbuf("l0rb", [128, 512], BF16, es=es)
    q_loaded = 0
    for sc in range(4):
        s0 = sc * 512
        ntt = 4 * sc + 4
        Tc.wait(V, V.n)
        qts = []
        for ft in range(16):
            t = P.dma("sync", qbuf[:, ft * 512:(ft + 1) * 512],
                      C["q_spill"][ft, :, s0:s0 + 512])
            qts.append(t)
        P.dma_wait(Tc, "sync", qts[-1])
        mask_cnt = G.n
        for tt in range(ntt):
            ps = singles[tt % 3]
            P.acquire(ps)
            for f in range(16):
                P.mm(ps[0:128, 0:512], ktile(f)[:, tt * 128:(tt + 1) * 128],
                     qbuf[:, f * 512:(f + 1) * 512], f == 0, f == 15,
                     inc=(f == 15))
            ps.ready = Tc.n
            V.wait(Tc, ps.ready)
            cv = V.op("tensor_copy", scM[:, tt * 512:(tt + 1) * 512],
                      ps[0:128, 0:512])
            ps.last_consumer = (V, cv)
            if tt >= 4 * sc:
                G.wait(V, cv)
                mask_cnt = G.op("affine_select", scM[:, tt * 512:(tt + 1) * 512],
                                scM[:, tt * 512:(tt + 1) * 512],
                                pattern=[[1, 512]], compare_op=OP.is_ge,
                                fill=0.0, base=s0 - tt * 128,
                                channel_multiplier=-1)
        ps4 = singles[3]
        P.acquire(ps4)
        Tc.wait(G, mask_cnt)
        Tc.wait(V, V.n)
        for tt in range(ntt):
            P.mm(ps4[0:1, 0:512], C["cst"]["col_ones128"],
                 scM[:, tt * 512:(tt + 1) * 512], tt == 0, tt == ntt - 1,
                 inc=(tt == ntt - 1))
        ps4.ready = Tc.n
        V.wait(Tc, ps4.ready)
        V.op("tensor_scalar", C["row_a"].sb[0:1, 0:512], ps4[0:1, 0:512],
             1e-6, None, OP.add)
        c2 = V.op("reciprocal", C["row_a"].sb[0:1, 0:512],
                  C["row_a"].sb[0:1, 0:512])
        ps4.last_consumer = (V, c2)
        P.acquire(ps4)
        Tc.wait(V, c2)
        P.mm(ps4[0:128, 0:512], C["cst"]["ones_row_f32"],
             C["row_a"].sb[0:1, 0:512], True, True, inc=True)
        ps4.ready = Tc.n
        V.wait(Tc, ps4.ready)
        c3 = V.op("tensor_copy", rb[:, 0:512], ps4[0:128, 0:512])
        ps4.last_consumer = (V, c3)
        for grp in range(4):
            P.acquire(pa)
            P.tw()
            for tt in range(ntt):
                vb = vbuf[:, (tt % 2) * 512:(tt % 2) * 512 + 512]
                key = ("l0v", tt % 2)
                stage_guard(P, C, key)
                vt = P.dma("sync", vb,
                           C["v_spill"][tt, :, grp * 512:(grp + 1) * 512])
                P.dma_wait(Tc, "sync", vt)
                for dvt in range(4):
                    P.mm(pa[:, dvt * 512:(dvt + 1) * 512],
                         vb[:, dvt * 128:(dvt + 1) * 128],
                         scM[:, tt * 512:(tt + 1) * 512], tt == 0,
                         tt == ntt - 1,
                         inc=(tt == ntt - 1 and dvt == 3))
                C["stage_consumer"][key] = None
            pa.ready = Tc.n
            C["stage_consumer"][("l0v", 0)] = (Tc, Tc.n)
            C["stage_consumer"][("l0v", 1)] = (Tc, Tc.n)
            V.wait(Tc, pa.ready)
            cc = 0
            for dvt in range(4):
                dv = grp * 4 + dvt
                cc = V.op("tensor_mul", qbuf[:, dv * 512:(dv + 1) * 512],
                          pa[:, dvt * 512:(dvt + 1) * 512], rb[:, 0:512])
            pa.last_consumer = (V, cc)
        P.tw()
        for o in range(NDT):
            ps = singles[o % 2]
            P.acquire(ps)
            for dv in range(16):
                P.mm(ps[0:128, 0:512],
                     woX.tile(dv)[:, o * 128:(o + 1) * 128],
                     qbuf[:, dv * 512:(dv + 1) * 512], dv == 0, dv == 15,
                     inc=(dv == 15))
            ps.ready = Tc.n
            key = ("x", 0)
            stage_guard(P, C, key)
            xt = P.dma("sync", C["xstage"].sb[:, 0:512],
                       x_in_dram[o * 128:(o + 1) * 128, s0:s0 + 512])
            keyg = ("l0g", o % 2)
            stage_guard(P, C, keyg)
            gt = P.dma("sync", gbufc[:, (o % 2) * 512:(o % 2) * 512 + 512],
                       C["g_spill"][o * 128:(o + 1) * 128, s0:s0 + 512])
            V.wait(Tc, ps.ready)
            t1 = V.op("tensor_scalar", C["scr_sq"].chunk(0, 0, 512),
                      ps[0:128, 0:512], boc[:, o:o + 1], None, OP.add)
            ps.last_consumer = (V, t1)
            P.dma_wait(V, "sync", gt)
            t2 = V.op("tensor_mul", C["scr_sq"].chunk(0, 0, 512),
                      C["scr_sq"].chunk(0, 0, 512),
                      gbufc[:, (o % 2) * 512:(o % 2) * 512 + 512])
            C["stage_consumer"][keyg] = (V, t2)
            P.dma_wait(V, "sync", xt)
            prev_store = C["xo_store"].get(0)
            if prev_store is not None:
                P.dma_wait(V, "sync", prev_store)
            t3 = V.op("tensor_tensor", C["xostage"].sb[:, 0:512],
                      C["xstage"].sb[:, 0:512], C["scr_sq"].chunk(0, 0, 512),
                      OP.add)
            C["stage_consumer"][key] = (V, t3)
            SY.wait(V, t3)
            st = P.dma("sync", x_out_dram[o * 128:(o + 1) * 128, s0:s0 + 512],
                       C["xostage"].sb[:, 0:512])
            C["xo_store"][0] = st
    P.barrier()
    es.close()



def emit_l0_ffn(P, C, li, xn, out_fm):
    V, S, Tc, G, SY = P.V, P.S, P.Tc, P.G, P.chains["sync"]
    W = C["W"]
    es = ExitStack()
    pa = C["ps_a"]
    singles = [C["ps1"], C["ps2"], C["ps3"], C["ps4"]]
    qbuf = C["s"]
    kbuf = out_fm
    bqc = colbuf(P, C, W["L0f_bqc"], 8, "f_bqc", es)
    bkc = colbuf(P, C, W["L0f_bkc"], 8, "f_bkc", es)
    ebqc = colbuf(P, C, W["L0f_ebqT"], 8, "f_ebq", es)
    ebkc = colbuf(P, C, W["L0f_ebkT"], 8, "f_ebk", es)
    for (wname, dstb, bc) in (("L0f_wqT", qbuf, bqc), ("L0f_wkT", kbuf, bkc)):
        wes = ExitStack()
        wq = load_weight_bf16(P, C, W[wname], D, 1024, wname, es=wes)

        def ep(o, c0, w, pb, dstb=dstb, bc=bc):
            evac_simple(P, C, pb, dstb.chunk(o, c0, w), bias_col=bc[:, o:o + 1])
        emit_proj_fm(P, C, wq, xn, D, ep)
        P.barrier()
        wes.close()
    wef = P.sbuf("l0wef", [128, 2 * 128], F32, es=es)
    web = P.sbuf("l0web", [128, 2 * 128], BF16, es=es)
    qe = P.sbuf("l0qe", [128, 2048], BF16, es=es)
    ke = P.sbuf("l0ke", [128, 2048], BF16, es=es)
    escr = P.sbuf("l0escr", [128, 2048], BF16, es=es)
    zcol = P.sbuf("l0z", [128, 1], F32, es=es)
    wcol = P.sbuf("l0w", [128, 1], BF16, es=es)
    negc4 = C["cst"]["c_negC4"]
    escr_consumed = 0
    web_consumed = 0
    for e in range(8):
        G.wait(Tc, web_consumed)
        t1 = P.dma("sync", wef[:, 0:128], W["L0f_eWqT"][e])
        t2 = P.dma("sync", wef[:, 128:256], W["L0f_eWkT"][e])
        P.dma_wait(G, "sync", t2)
        cw = G.op("tensor_copy", web[:], wef[:])
        for h in range(8):
            for (srcb, dst, bcol, woff) in (
                    (qbuf, qe, ebqc[:, e:e + 1], 0),
                    (kbuf, ke, ebkc[:, e:e + 1], 128)):
                P.tw()
                P.acquire(pa)
                for j in range(0, T, 512):
                    P.mm(pa[:, j:j + 512], web[:, woff:woff + 128],
                         srcb.chunk(h, j, 512), True, True,
                         inc=(j == T - 512))
                pa.ready = Tc.n
                evac_simple(P, C, pb=pa, dst_ap=dst[:, 0:T], bias_col=bcol)
            web_consumed = Tc.n
            for st in range(NTT):
                P.tw()
                P.acquire(pa)
                for j in range(0, T, 512):
                    P.mm(pa[:, j:j + 512], qe[:, st * 128:(st + 1) * 128],
                         ke[:, j:j + 512], True, True, inc=(j == T - 512))
                pa.ready = Tc.n
                S.wait(Tc, pa.ready)
                S.wait(Tc, escr_consumed)
                ca = S.op("activation", escr[:], pa[:, 0:T], AF.Exp,
                          bias=negc4[0:128, 0:1], scale=0.0883883,
                          accum_out=zcol[:])
                pa.last_consumer = (S, ca)
                V.wait(S, ca)
                V.op("reciprocal", wcol[:] if False else zcol[:], zcol[:])
                cz = V.op("tensor_scalar", wcol[:], zcol[:], 1.0 / 16384.0,
                          None, OP.mult)
                Tc.wait(V, cz)
                first = (h == 0 and st == 0)
                for j in range(4):
                    P.mm(singles[j][0:1, 0:512], wcol[:],
                         escr[:, j * 512:(j + 1) * 512], first, False,
                         inc=(j == 3))
                escr_consumed = Tc.n
        for j in range(4):
            singles[j].ready = Tc.n
        V.wait(Tc, Tc.n)
        cr = 0
        for j in range(4):
            cr = V.op("tensor_copy", C["row_a"].sb[0:1, j * 512:(j + 1) * 512],
                      singles[j][0:1, 0:512])
            singles[j].last_consumer = (V, cr)
        SY.wait(V, cr)
        P.dma("sync", C["l0_scores"][e:e + 1, :], C["row_a"].sb[0:1, 0:T])
        P.barrier()
    srows = P.sbuf("l0srows", [8, 2048], F32, es=es)
    mtm = P.sbuf("l0mtm", [128, 16 * 8], F32, es=es)
    stm = P.sbuf("l0stm", [128, 16 * 8], F32, es=es)
    mfm = P.sbuf("l0mfm", [8, 2048], BF16, es=es)
    mcol = P.sbuf("l0mc", [128, 1], F32, es=es)
    tsr = P.dma("sync", srows[:], C["l0_scores"])
    P.dma_wait(Tc, "sync", tsr)
    identf = C["cst"]["id128_f32"]
    ps1 = singles[0]
    for tt in range(NTT):
        P.acquire(ps1)
        Tc.op("transpose", ps1[0:128, 0:8], srows[0:8, tt * 128:(tt + 1) * 128],
              identf[0:8, 0:8], self_wait=False, inc=True)
        ps1.ready = Tc.n
        V.wait(Tc, ps1.ready)
        c = V.op("tensor_copy", stm[:, tt * 8:(tt + 1) * 8], ps1[0:128, 0:8])
        ps1.last_consumer = (V, c)
    for tt in range(NTT):
        tv = stm[:, tt * 8:(tt + 1) * 8]
        V.op("tensor_reduce", mcol[:], tv, AX.X, OP.max)
        V.op("tensor_scalar", mtm[:, tt * 8:(tt + 1) * 8], tv, mcol[:], None,
             OP.is_ge)
    Tc.wait(V, V.n)
    for tt in range(NTT):
        P.acquire(ps1)
        Tc.op("transpose", ps1[0:8, 0:128], mtm[:, tt * 8:(tt + 1) * 8],
              identf, self_wait=False, inc=True)
        ps1.ready = Tc.n
        V.wait(Tc, ps1.ready)
        c = V.op("tensor_copy", mfm[0:8, tt * 128:(tt + 1) * 128],
                 ps1[0:8, 0:128])
        ps1.last_consumer = (V, c)
    P.barrier()
    bexp = load_weight_bf16(P, C, W["L0f_bexp"], 8, 1024, "l0bexp", es=es)
    P.tw()
    for o in range(NDT):
        P.acquire(pa)
        for j in range(0, T, 512):
            P.mm(pa[:, j:j + 512], bexp.tile(0)[0:8, o * 128:(o + 1) * 128],
                 mfm[0:8, j:j + 512], True, True, inc=(j == T - 512))
        pa.ready = Tc.n
        c = evac_simple(P, C, pa, out_fm.tile(o))
    P.barrier()
    wslot = FM(P, "l0fw", D, BF16, tokens=1024, es=es)
    Bg = C["scr_b1"]
    eo_done = 0
    for e in range(8):
        load_weight_bf16(P, C, W["L0f_Wexp"][e], D, 1024, f"l0we{e}",
                         dst=wslot, guard=(Tc, eo_done))
        P.tw()
        for j in range(0, T, 512):
            pb = singles[1 + (j // 512) % 2]
            P.acquire(pb)
            P.mm(pb[0:128, 0:512], C["cst"]["eyerep8"][0:8, e * 128:(e + 1) * 128],
                 mfm[0:8, j:j + 512], True, True, inc=True)
            pb.ready = Tc.n
            V.wait(Tc, pb.ready)
            c = V.op("tensor_copy", Bg.sb[:, j:j + 512], pb[0:128, 0:512])
            pb.last_consumer = (V, c)

        def ep_eo(o, c0, w, pb):
            V.wait(Tc, pb.ready)
            t1 = V.op("tensor_mul", C["scr_sq"].chunk(0, 0, w), pb[:, 0:w],
                      Bg.sb[:, c0:c0 + w])
            pb.last_consumer = (V, t1)
            V.op("tensor_add", out_fm.chunk(o, c0, w), out_fm.chunk(o, c0, w),
                 C["scr_sq"].chunk(0, 0, w))
        emit_proj_fm(P, C, wslot, xn, D, ep_eo)
        eo_done = Tc.n
    P.barrier()
    es.close()



def emit_l3_attn(P, C, li, xn, out_fm, x_in_dram, x_out_dram):
    V, S, Tc, SY = P.V, P.S, P.Tc, P.chains["sync"]
    W = C["W"]
    es = ExitStack()
    gbuf, sbuf_ = C["g"], C["s"]
    SY.wait(S, S.n)
    SY.wait(V, V.n)
    g_spill_tgts = []
    for i in range(NDT):
        t = P.dma("sync", C["g_spill"][i * 128:(i + 1) * 128, :], gbuf.tile(i))
        g_spill_tgts.append(t)
    gs_done = g_spill_tgts[-1]
    qbuf = FM(P, "l3q", D, BF16, es=es)
    wpiece_es = ExitStack()
    wq = load_weight_bf16(P, C, W["L3t_wqkv"][:, 0:1024], D, 1024, "l3wq",
                          es=wpiece_es)

    def ep_q(o, c0, w, pb):
        c = evac_simple(P, C, pb, qbuf.chunk(o, c0, w))
    emit_proj_fm(P, C, wq, xn, D, ep_q)
    P.barrier()
    wpiece_es.close()
    wpiece_es = ExitStack()
    wk = load_weight_bf16(P, C, W["L3t_wqkv"][:, 1024:2048], D, 1024, "l3wk",
                          es=wpiece_es)
    V.wait("dma_sync", gs_done)

    def ep_k(o, c0, w, pb):
        c = evac_simple(P, C, pb, gbuf.chunk(o, c0, w))
    emit_proj_fm(P, C, wk, xn, D, ep_k)
    P.barrier()
    wpiece_es.close()
    wpiece_es = ExitStack()
    wv = load_weight_bf16(P, C, W["L3t_wqkv"][:, 2048:3072], D, 1024, "l3wv",
                          es=wpiece_es)
    pa = C["ps_a"]
    P.tw()
    for tt in range(NTT):
        P.acquire(pa)
        for k in range(NDT):
            lhsT = xn.tile(k)[:, tt * 128:(tt + 1) * 128]
            for j in range(0, 1024, 512):
                P.mm(pa[:, j:j + 512], lhsT, wv.tile(k)[:, j:j + 512],
                     k == 0, k == NDT - 1,
                     inc=(k == NDT - 1 and j == 512))
        pa.ready = Tc.n
        V.wait(Tc, pa.ready)
        c = V.op("tensor_copy", sbuf_.sb[:, tt * 1024:(tt + 1) * 1024],
                 pa[:, 0:1024])
        pa.last_consumer = (V, c)
    P.barrier()
    wpiece_es.close()
    expT = P.sbuf("l3expT", [128, 2048], BF16, es=es)
    ps1, ps2 = C["ps1"], C["ps2"]
    onesc = C["cst"]["col_ones128"]
    negc = C["cst"]["c_negC12"]
    rowz = C["row_a"]
    rzb = C["scr_b1"]
    expT_consumed = 0
    for sc in range(4):
        s0 = sc * 512
        for h in range(16):
            kt = h // 2
            kb = (h % 2) * 64
            P.acquire(ps1)
            for r in range(4):
                P.acquire(pa)
                for b in range(4):
                    tt = 4 * r + b
                    lhsT = gbuf.tile(kt)[kb:kb + 64, tt * 128:(tt + 1) * 128]
                    rhs = qbuf.tile(kt)[kb:kb + 64, s0:s0 + 512]
                    P.mm(pa[:, b * 512:(b + 1) * 512], lhsT, rhs, True, True,
                         inc=(b == 3))
                pa.ready = Tc.n
                S.wait(Tc, pa.ready)
                S.wait(Tc, expT_consumed)
                ca = S.op("activation", expT[:], pa[:, 0:2048], AF.Exp,
                          bias=negc[0:128, 0:1], scale=0.125)
                pa.last_consumer = (S, ca)
                Tc.wait(S, ca)
                for b in range(4):
                    tt = 4 * r + b
                    P.mm(ps1[0:64, 0:512],
                         sbuf_.sb[:, tt * 1024 + h * 64: tt * 1024 + h * 64 + 64],
                         expT[:, b * 512:(b + 1) * 512], tt == 0, tt == 15)
                    P.mm(ps1[64:65, 0:512], onesc, expT[:, b * 512:(b + 1) * 512],
                         tt == 0, tt == 15, inc=(b == 3))
                expT_consumed = Tc.n
            ps1.ready = Tc.n
            V.wait(Tc, ps1.ready)
            c1 = V.op("tensor_copy", rowz.sb[0:1, 0:512], ps1[64:65, 0:512])
            c2 = V.op("reciprocal", rowz.sb[0:1, 0:512], rowz.sb[0:1, 0:512])
            P.acquire(ps2)
            Tc.wait(V, c2)
            P.mm(ps2[0:64, 0:512], C["cst"]["ones_row_f32"][0:1, 0:64],
                 rowz.sb[0:1, 0:512], True, True, inc=True)
            ps2.ready = Tc.n
            V.wait(Tc, ps2.ready)
            c3 = V.op("tensor_copy", rzb.sb[0:64, 0:512], ps2[0:64, 0:512])
            ps2.last_consumer = (V, c3)
            c4 = V.op("tensor_mul", xn.tile(kt)[kb:kb + 64, s0:s0 + 512],
                      ps1[0:64, 0:512], rzb.sb[0:64, 0:512])
            ps1.last_consumer = (V, c4)
    P.barrier()
    es.close()
    es2 = ExitStack()
    for i in range(NDT):
        t = P.dma("sync", gbuf.tile(i), C["g_spill"][i * 128:(i + 1) * 128, :])
    greload = P.dma_counts["sync"]
    V.wait("dma_sync", greload)
    wo = load_weight_bf16(P, C, W["L3t_wout"], D, 1024, "l3wo", es=es2)

    def ep_out(o, c0, w, pb):
        key = ("x", 0)
        stage_guard(P, C, key)
        tgt = P.dma("sync", C["xstage"].tile(0),
                    x_in_dram[o * 128:(o + 1) * 128, :])
        V.wait(Tc, pb.ready)
        t1 = V.op("tensor_copy", C["scr_sq"].chunk(0, 0, w), pb[:, 0:w])
        pb.last_consumer = (V, t1)
        V.op("tensor_mul", C["scr_sq"].chunk(0, 0, w),
             C["scr_sq"].chunk(0, 0, w), gbuf.tile(o))
        P.dma_wait(V, "sync", tgt)
        prev_store = C["xo_store"].get(0)
        if prev_store is not None:
            P.dma_wait(V, "sync", prev_store)
        ca = V.op("tensor_tensor", C["xostage"].tile(0),
                  C["xstage"].tile(0), C["scr_sq"].chunk(0, 0, w), OP.add)
        C["stage_consumer"][key] = (V, ca)
        SY.wait(V, ca)
        st = P.dma("sync", x_out_dram[o * 128:(o + 1) * 128, :],
                   C["xostage"].tile(0))
        C["xo_store"][0] = st

    emit_proj_fm(P, C, wo, xn, D, ep_out)
    P.barrier()
    es2.close()



def emit_l3_ffn(P, C, li, xn, out_fm):
    V, S, Tc, G = P.V, P.S, P.Tc, P.G
    W = C["W"]
    es = ExitStack()
    hbuf = C["s"]
    pa = C["ps_a"]
    ps1, ps2, ps3 = C["ps1"], C["ps2"], C["ps3"]
    bs1 = colbuf(P, C, W["L3f_bs1c"], 8, "bs1", es)
    bs2 = colbuf(P, C, W["L3f_bs2c"], 8, "bs2", es)
    wes = ExitStack()
    ws1 = load_weight_bf16(P, C, W["L3f_ws1"], D, 1024, "ws1", es=wes)

    def ep_h1(o, c0, w, pb):
        evac_simple(P, C, pb, hbuf.chunk(o, c0, w), bias_col=bs1[:, o:o + 1],
                    act=AF.Gelu, eng="scalar")
    emit_proj_fm(P, C, ws1, xn, D, ep_h1)
    P.barrier()
    wes.close()
    wes = ExitStack()
    ws2 = load_weight_bf16(P, C, W["L3f_ws2"], D, 1024, "ws2", es=wes)

    def ep_sh(o, c0, w, pb):
        V.wait(Tc, pb.ready)
        c = V.op("tensor_scalar", out_fm.chunk(o, c0, w), pb[:, 0:w],
                 bs2[:, o:o + 1], None, OP.add)
        c = V.op("tensor_add", out_fm.chunk(o, c0, w), out_fm.chunk(o, c0, w),
                 xn.chunk(o, c0, w))
        pb.last_consumer = (V, c)
    emit_proj_fm(P, C, ws2, hbuf, D, ep_sh)
    P.barrier()
    wes.close()
    gates_fm = P.sbuf("l3gfm", [31, 2048], BF16, es=es)
    res = ExitStack()
    wr = load_weight_bf16(P, C, W["L3f_wr"], D, 31, "wr", es=res)
    brc = colbuf(P, C, W["L3f_brc"], 1, "brc", res, p=31)
    rl = P.sbuf("l3rl", [31, 2048], F32, es=res)
    P.tw()
    P.acquire(pa)
    for k in range(NDT):
        lhsT = wr.tile(k)[:, 0:31]
        for j in range(0, T, 512):
            P.mm(pa[0:31, j:j + 512], lhsT, xn.chunk(k, j, 512), k == 0,
                 k == NDT - 1, inc=(k == NDT - 1 and j == T - 512))
    pa.ready = Tc.n
    V.wait(Tc, pa.ready)
    cb = V.op("tensor_scalar", rl[0:31, 0:T], pa[0:31, 0:T], brc[0:31, 0:1],
              None, OP.add)
    pa.last_consumer = (V, cb)
    rltm = P.sbuf("l3rltm", [128, 16 * 31], F32, es=res)
    gates_tm = P.sbuf("l3gtm", [128, 16 * 31], F32, es=res)
    ident = C["cst"]["id128_bf"]
    identf = C["cst"]["id128_f32"]
    Tc.wait(V, V.n)
    for tt in range(NTT):
        P.acquire(ps1)
        Tc.op("transpose", ps1.ap[0:128, 0:31], rl[0:31, tt * 128:(tt + 1) * 128],
              identf[0:31, 0:31], self_wait=False, inc=True)
        ps1.ready = Tc.n
        V.wait(Tc, ps1.ready)
        c = V.op("tensor_copy", rltm[:, tt * 31:(tt + 1) * 31],
                 ps1.ap[0:128, 0:31])
        ps1.last_consumer = (V, c)
    aff = P.sbuf("l3aff", [128, 31], F32, es=res)
    wrk = P.sbuf("l3wrk", [128, 31], F32, es=res)
    mcol = P.sbuf("l3m", [128, 1], F32, es=res)
    zcol = P.sbuf("l3z", [128, 1], F32, es=res)
    for tt in range(NTT):
        tv = rltm[:, tt * 31:(tt + 1) * 31]
        V.op("tensor_reduce", mcol[:], tv, AX.X, OP.max, negate=True)
        S.wait(V, V.n)
        ce = S.op("activation", aff[:], tv, AF.Exp, bias=mcol[:],
                  accum_out=zcol[:])
        V.wait(S, ce)
        V.op("reciprocal", zcol[:], zcol[:])
        V.op("tensor_scalar", aff[:], aff[:], zcol[:], None, OP.mult)
        V.op("tensor_copy", wrk[:], aff[:])
        for it in range(3):
            V.op("tensor_reduce", mcol[:], wrk[:], AX.X, OP.max)
            V.op("tensor_scalar", mcol[:] if False else wrk[:], wrk[:],
                 mcol[:], None, OP.is_ge) if False else None
            V.op("tensor_scalar", C["scr_b2"].sb[0:128, 0:31], wrk[:],
                 mcol[:], None, OP.is_ge)
            V.op("scalar_tensor_tensor", wrk[:], C["scr_b2"].sb[0:128, 0:31],
                 -9.0, wrk[:], OP.mult, OP.add)
        V.op("tensor_tensor", wrk[:], aff[:], wrk[:], OP.subtract)
        V.op("tensor_scalar", wrk[:], wrk[:], 1.0 / 9.0, None, OP.mult)
        V.op("tensor_tensor", gates_tm[:, tt * 31:(tt + 1) * 31], wrk[:],
             aff[:], OP.mult)
    Tc.wait(V, V.n)
    for tt in range(NTT):
        P.acquire(ps1)
        Tc.op("transpose", ps1.ap[0:31, 0:128],
              gates_tm[:, tt * 31:(tt + 1) * 31], identf, self_wait=False,
              inc=True)
        ps1.ready = Tc.n
        V.wait(Tc, ps1.ready)
        c = V.op("tensor_copy", gates_fm[0:31, tt * 128:(tt + 1) * 128],
                 ps1.ap[0:31, 0:128])
        ps1.last_consumer = (V, c)
    P.barrier()
    res.close()
    be2all = load_weight_bf16(P, C, W["L3f_Be2"], 31, 1024, "be2all", es=es)
    P.tw()
    for o in range(NDT):
        P.acquire(pa)
        for j in range(0, T, 512):
            P.mm(pa[:, j:j + 512], be2all.tile(0)[0:31, o * 128:(o + 1) * 128],
                 gates_fm[0:31, j:j + 512], True, True,
                 inc=(j == T - 512))
        pa.ready = Tc.n
        V.wait(Tc, pa.ready)
        c = V.op("tensor_add", out_fm.tile(o), out_fm.tile(o), pa[:, 0:T])
        pa.last_consumer = (V, c)
    w1slot = FM(P, "l3w1slot", D, BF16, tokens=1024, es=es)
    w2slot = FM(P, "l3w2slot", D, BF16, tokens=1024, es=es)
    h_mms_done = 0
    eo_mms_done = 0
    Bg = C["scr_b1"]
    eyerep = C["cst"]["eyerep31"]
    be1 = P.sbuf("l3be1", [128, 8], F32, es=es)
    be1_consumed = 0
    for e in range(31):
        P.chains["sync"].wait(P.S, be1_consumed)
        bt = P.dma("sync", be1[:], W["L3f_Be1c"][e])
        P.dma_wait(P.S, "sync", bt)
        load_weight_bf16(P, C, W["L3f_We1"][e], D, 1024, f"we1_{e}",
                         dst=w1slot, guard=(Tc, h_mms_done))
        P.tw()
        for j in range(0, T, 512):
            pb = (ps2, ps3)[(j // 512) % 2]
            P.acquire(pb)
            P.mm(pb[0:64, 0:512], eyerep[0:31, e * 64:(e + 1) * 64],
                 gates_fm[0:31, j:j + 512], True, True)
            P.mm(pb[64:128, 0:512], eyerep[0:31, e * 64:(e + 1) * 64],
                 gates_fm[0:31, j:j + 512], True, True, inc=True)
            pb.ready = Tc.n
            V.wait(Tc, pb.ready)
            c = V.op("tensor_copy", Bg.sb[:, j:j + 512], pb[0:128, 0:512])
            pb.last_consumer = (V, c)

        def ep_he(o, c0, w, pb, be1=be1):
            evac_simple(P, C, pb, be1[0:128, o:o + 1] if False else
                        hbuf.chunk(o, c0, w),
                        bias_col=be1[0:128, o:o + 1], act=AF.Gelu, eng="scalar")
        emit_proj_fm(P, C, w1slot, xn, D, ep_he)
        h_mms_done = Tc.n
        be1_consumed = P.S.n
        load_weight_bf16(P, C, W["L3f_We2"][e], D, 1024, f"we2_{e}",
                         dst=w2slot, guard=(Tc, eo_mms_done))

        def ep_eo(o, c0, w, pb):
            V.wait(Tc, pb.ready)
            t1 = V.op("tensor_mul", C["scr_sq"].chunk(0, 0, w), pb[:, 0:w],
                      Bg.sb[:, c0:c0 + w])
            pb.last_consumer = (V, t1)
            V.op("tensor_add", out_fm.chunk(o, c0, w), out_fm.chunk(o, c0, w),
                 C["scr_sq"].chunk(0, 0, w))
        emit_proj_fm(P, C, w2slot, hbuf, D, ep_eo)
        eo_mms_done = Tc.n
    P.barrier()
    es.close()



def mm_transpose(P, C, in_ap, out_cb, kdim, idim, ident):
    raise NotImplementedError


def _l2dbg(P, C, name, ap, p, n):
    if not CONFIG.get("l2dbg"):
        return
    dap = C["nc_handle"].dram_tensor("l2dbg_" + name, [p, n], F32,
                                     kind="ExternalOutput").ap()
    V = P.V
    c = V.op("tensor_copy", C["xstage"].sb[0:p, 0:n], ap)
    P.chains["sync"].wait(V, c)
    st = P.dma("sync", dap, C["xstage"].sb[0:p, 0:n])
    P.dma_wait(V, "sync", st)
    C["stage_consumer"][("x", 0)] = (V, c)


def emit_l2_ffn(P, C, li, xn, out_fm):
    V, S, Tc, G, SY = P.V, P.S, P.Tc, P.G, P.chains["sync"]
    W = C["W"]
    NS = 64
    es = ExitStack()
    pa = C["ps_a"]
    singles = [C["ps1"], C["ps2"], C["ps3"], C["ps4"]]
    identb = C["cst"]["id128_bf"]
    identf = C["cst"]["id128_f32"]
    onesrow_f = C["cst"]["ones_row_f32"]
    onesrow_b = C["cst"]["ones_row_bf16"]
    hbuf = C["s"]
    sinitT = load_weight_bf16(P, C, W["L2f_sinitT"], D, NS, "sinitT", es=es)
    msT = P.sbuf("l2msT", [128, 8], BF16, es=es)
    msTf = P.sbuf("l2msTf", [128, 8], F32, es=es)
    V.wait(P.G, P.G.n)
    for k in range(NDT):
        V.op("tensor_reduce", msTf[:, k:k + 1], sinitT.tile(k), AX.X, OP.add)
    V.op("tensor_scalar", msT[:], msTf[:], 1.0 / NS, None, OP.mult)
    wes = ExitStack()
    b1rowb = P.sbuf("l2b1rb", [1, 512], BF16, es=wes)
    load_row_bf16(P, C, W["L2f_b1row"], 512, b1rowb[0:1, 0:512])
    w1a = load_weight_bf16(P, C, W["L2f_w1a"], D, 512, "w1a", es=wes)
    w1b = load_weight_bf16(P, C, W["L2f_w1b"], D, 512, "w1b", es=wes)
    ps1 = singles[0]
    P.tw()
    P.acquire(ps1)
    for k in range(NDT):
        P.mm(ps1[0:1, 0:512], msT[:, k:k + 1], w1b.tile(k), k == 0, False)
    P.mm(ps1[0:1, 0:512], onesrow_b[0:1, 0:1], b1rowb[0:1, 0:512], False, True,
         inc=True)
    ps1.ready = Tc.n
    termr = P.sbuf("l2term", [1, 512], F32, es=wes)
    V.wait(Tc, ps1.ready)
    ct = V.op("tensor_copy", termr[:], ps1[0:1, 0:512])
    ps1.last_consumer = (V, ct)
    termc = P.sbuf("l2termc", [128, 4], F32, es=wes)
    Tc.wait(V, ct)
    for c in range(4):
        P.acquire(ps1)
        P.mm(ps1[0:128, 0:1], termr[0:1, c * 128:(c + 1) * 128],
             onesrow_f[0:1, 0:1], True, True, inc=True)
        ps1.ready = Tc.n
        V.wait(Tc, ps1.ready)
        cc = V.op("tensor_copy", termc[:, c:c + 1], ps1[0:128, 0:1])
        ps1.last_consumer = (V, cc)

    _l2dbg(P, C, "termc", termc[0:128, 0:4], 128, 4)

    def ep_h(o, c0, w, pb):
        evac_simple(P, C, pb, hbuf.chunk(o, c0, w), bias_col=termc[:, o:o + 1],
                    act=AF.Gelu, eng="scalar")
    emit_proj_fm(P, C, w1a, xn, 512, ep_h)
    P.barrier()
    wes.close()
    alpha_b = P.sbuf("l2alphab", [128, 16 * 64], BF16, es=es)
    alphaT = P.sbuf("l2alphaT", [64, 2048], BF16, es=es)
    wes = ExitStack()
    w2 = load_weight_bf16(P, C, W["L2f_w2"], 512, NS, "w2", es=wes)
    b2c = colbuf(P, C, W["L2f_b2c"], 1, "b2c", wes, p=64)
    taub = P.sbuf("l2tau", [1, 1], F32, es=wes)
    rtauc = P.sbuf("l2rtau", [64, 1], F32, es=wes)
    tt_ = P.dma("sync", taub[:], W["L2f_tau"])
    P.dma_wait(S, "sync", tt_)
    S.op("activation", taub[:], taub[:], AF.Abs)
    V.wait(S, S.n)
    V.op("tensor_scalar", taub[:], taub[:], 0.1, None, OP.add)
    cv = V.op("reciprocal", taub[:], taub[:])
    P.acquire(singles[1])
    Tc.wait(V, cv)
    P.mm(singles[1][0:64, 0:1], onesrow_f[0:1, 0:64], taub[0:1, 0:1], True,
         True, inc=True)
    singles[1].ready = Tc.n
    V.wait(Tc, singles[1].ready)
    cr = V.op("tensor_copy", rtauc[:], singles[1][0:64, 0:1])
    singles[1].last_consumer = (V, cr)
    rl = P.sbuf("l2rl", [64, 2048], F32, es=wes)
    P.tw()
    P.acquire(pa)
    for k in range(4):
        lhsT = w2.tile(k)[0:128, 0:64]
        for j in range(0, T, 512):
            P.mm(pa[0:64, j:j + 512], lhsT, hbuf.chunk(k, j, 512), k == 0,
                 k == 3, inc=(k == 3 and j == T - 512))
    pa.ready = Tc.n
    V.wait(Tc, pa.ready)
    crl = V.op("tensor_scalar", rl[0:64, 0:T], pa[0:64, 0:T], b2c[0:64, 0:1],
               rtauc[0:64, 0:1], OP.add, OP.mult)
    pa.last_consumer = (V, crl)
    _l2dbg(P, C, "rl", rl[0:64, 0:1024], 64, 1024)
    _l2dbg(P, C, "hh", hbuf.sb[0:128, 0:1024], 128, 1024)
    res = ExitStack()
    rltm = P.sbuf("l2rltm", [128, 16 * 64], F32, es=res)
    Tc.wait(V, crl)
    for tc4 in range(4):
        P.acquire(pa)
        for c in range(4):
            tt = tc4 * 4 + c
            P.mm(pa[0:128, c * 512:c * 512 + 64],
                 rl[0:64, tt * 128:(tt + 1) * 128], identf[0:64, 0:64],
                 True, True, inc=(c == 3))
        pa.ready = Tc.n
        V.wait(Tc, pa.ready)
        cc = 0
        for c in range(4):
            tt = tc4 * 4 + c
            cc = V.op("tensor_copy", rltm[:, tt * 64:(tt + 1) * 64],
                      pa[0:128, c * 512:c * 512 + 64])
        pa.last_consumer = (V, cc)
    aff = P.sbuf("l2aff", [128, 64], F32, es=res)
    wrk = P.sbuf("l2wrk", [128, 64], F32, es=res)
    ind = P.sbuf("l2ind", [128, 64], F32, es=res)
    mcol = P.sbuf("l2m", [128, 1], F32, es=res)
    zcol = P.sbuf("l2z", [128, 1], F32, es=res)
    for tt in range(NTT):
        tv = rltm[:, tt * 64:(tt + 1) * 64]
        V.op("tensor_reduce", mcol[:], tv, AX.X, OP.max, negate=True)
        S.wait(V, V.n)
        ce = S.op("activation", aff[:], tv, AF.Exp, bias=mcol[:])
        V.wait(S, ce)
        V.op("tensor_copy", wrk[:], tv)
        V.op("memset", ind[:], 0.0)
        for it in range(8):
            V.op("tensor_reduce", mcol[:], wrk[:], AX.X, OP.max)
            V.op("tensor_scalar", C["scr_b2"].sb[0:128, 0:64], wrk[:],
                 mcol[:], None, OP.is_ge)
            V.op("tensor_add", ind[:], ind[:], C["scr_b2"].sb[0:128, 0:64])
            V.op("scalar_tensor_tensor", wrk[:], C["scr_b2"].sb[0:128, 0:64],
                 -1e9, wrk[:], OP.mult, OP.add)
        V.op("tensor_mul", aff[:], aff[:], ind[:])
        V.op("tensor_reduce", zcol[:], aff[:], AX.X, OP.add)
        V.op("reciprocal", zcol[:], zcol[:])
        V.op("tensor_scalar", alpha_b[:, tt * 64:(tt + 1) * 64], aff[:],
             zcol[:], None, OP.mult)
    P.barrier()
    res.close()
    wes.close()
    for tt in range(NTT):
        ps = singles[tt % 2]
        P.acquire(ps)
        P.mm(ps[0:64, 0:128], alpha_b[:, tt * 64:(tt + 1) * 64], identb,
             True, True, inc=True)
        ps.ready = Tc.n
        V.wait(Tc, ps.ready)
        cc = V.op("tensor_copy", alphaT[0:64, tt * 128:(tt + 1) * 128],
                  ps[0:64, 0:128])
        ps.last_consumer = (V, cc)
    xn_tm = FMView(out_fm.sb, T, 1024)
    for tt in range(NTT):
        P.acquire(pa)
        for k in range(NDT):
            bank = k % 4
            P.mm(pa[0:128, bank * 512 + (k // 4) * 128:
                    bank * 512 + (k // 4) * 128 + 128],
                 xn.tile(k)[:, tt * 128:(tt + 1) * 128], identb,
                 True, True, inc=(k == NDT - 1))
        pa.ready = Tc.n
        V.wait(Tc, pa.ready)
        cc = 0
        for k in range(NDT):
            bank = k % 4
            cc = V.op("tensor_copy",
                      xn_tm.tile(tt)[:, k * 128:(k + 1) * 128],
                      pa[0:128, bank * 512 + (k // 4) * 128:
                         bank * 512 + (k // 4) * 128 + 128])
        pa.last_consumer = (V, cc)
    P.tw()
    P.acquire(pa, singles[2])
    for tt in range(NTT):
        lhsT = alpha_b[:, tt * 64:(tt + 1) * 64]
        for j in range(0, 1024, 512):
            P.mm(pa[0:64, j:j + 512], lhsT, xn_tm.tile(tt)[:, j:j + 512],
                 tt == 0, tt == NTT - 1)
        P.mm(singles[2][0:64, 0:1], lhsT, C["cst"]["col_ones128"],
             tt == 0, tt == NTT - 1, inc=(tt == NTT - 1))
    pa.ready = singles[2].ready = Tc.n
    _l2dbg(P, C, "alpha", alpha_b[:, 0:1024], 128, 1024)
    rden = P.sbuf("l2rden", [64, 1], F32, es=es)
    V.wait(Tc, Tc.n)
    V.op("tensor_scalar", rden[:], singles[2][0:64, 0:1], 1e-8, None, OP.add)
    cd = V.op("reciprocal", rden[:], rden[:])
    singles[2].last_consumer = (V, cd)
    si = P.sbuf("l2si", [64, 1024], BF16, es=es)
    cs = V.op("tensor_scalar", si[:], pa[0:64, 0:1024], rden[:], None, OP.mult)
    pa.last_consumer = (V, cs)
    _l2dbg(P, C, "si", si[0:64, 0:1024], 64, 1024)
    def transpose_64(src_ap_fn, dst, nt):
        for k in range(nt):
            ps = singles[k % 2]
            P.acquire(ps)
            P.mm(ps[0:128, 0:64], src_ap_fn(k), identb[0:64, 0:64], True, True,
                 inc=True)
            ps.ready = Tc.n
            V.wait(Tc, ps.ready)
            cc = V.op("tensor_copy", dst[:, k * 64:(k + 1) * 64],
                      ps[0:128, 0:64])
            ps.last_consumer = (V, cc)

    rr = P.sbuf("l2rr", [64, 1024], BF16, es=es)
    zz = P.sbuf("l2zz", [64, 1024], BF16, es=es)
    nn = P.sbuf("l2nn", [64, 1024], BF16, es=es)
    snew = P.sbuf("l2snew", [64, 1024], BF16, es=es)
    sinit_nat = load_weight_bf16(P, C, W["L2f_sinit"], NS, 1024, "sinit", es=es)
    gru_es = ExitStack()
    siT = P.sbuf("l2siT", [128, 8 * 64], BF16, es=gru_es)
    Tc.wait(V, V.n)
    transpose_64(lambda k: si[0:64, k * 128:(k + 1) * 128], siT, 8)
    gi = P.sbuf("l2gi", [64, 3072], BF16, es=gru_es)
    gh = P.sbuf("l2gh", [64, 3072], BF16, es=gru_es)
    bihrow_b = P.sbuf("l2bihb", [1, 3072], BF16, es=gru_es)
    wih_slot = FM(P, "l2wih", D, BF16, tokens=512, es=gru_es)
    wconsumed = 0
    brow_consumed = [0]
    for (wname, srcT, dstg, bname) in (("L2f_wihT", siT, gi, "L2f_bihrow"),
                                       ("L2f_whhT", None, gh, "L2f_bhhrow")):
        P.chains["sync"].wait(Tc, brow_consumed[0])
        load_row_bf16(P, C, W[bname], 3072, bihrow_b[0:1, 0:3072])
        for ch in range(6):
            load_weight_bf16(P, C, W[wname][:, ch * 512:(ch + 1) * 512], D,
                             512, "wih", dst=wih_slot, guard=(Tc, wconsumed))
            P.tw()
            ps = singles[2]
            P.acquire(ps)
            for k in range(NDT):
                lhsT = (srcT[:, k * 64:(k + 1) * 64] if srcT is not None
                        else sinitT.tile(k))
                P.mm(ps[0:64, 0:512], lhsT, wih_slot.tile(k), k == 0, False)
            P.mm(ps[0:64, 0:512], onesrow_b[0:1, 0:64],
                 bihrow_b[0:1, ch * 512: ch * 512 + 512],
                 False, True, inc=True)
            wconsumed = Tc.n
            brow_consumed[0] = Tc.n
            ps.ready = Tc.n
            V.wait(Tc, ps.ready)
            cc = V.op("tensor_copy", dstg[0:64, ch * 512:(ch + 1) * 512],
                      ps[0:64, 0:512])
            ps.last_consumer = (V, cc)
    V.wait(P.G, P.G.n)
    V.op("tensor_add", rr[:], gi[0:64, 0:1024], gh[0:64, 0:1024])
    V.op("tensor_add", zz[:], gi[0:64, 1024:2048], gh[0:64, 1024:2048])
    cvv = V.n
    S.wait(V, cvv)
    S.op("activation", rr[:], rr[:], AF.Sigmoid)
    sz = S.op("activation", zz[:], zz[:], AF.Sigmoid)
    V.wait(S, sz)
    V.op("tensor_mul", nn[:], rr[:], gh[0:64, 2048:3072])
    V.op("tensor_add", nn[:], nn[:], gi[0:64, 2048:3072])
    cvv = V.n
    S.wait(V, cvv)
    sn = S.op("activation", nn[:], nn[:], AF.Tanh)
    V.wait(S, sn)
    V.op("tensor_mul", rr[:], zz[:], nn[:])
    V.op("tensor_sub", nn[:], nn[:], rr[:])
    V.op("tensor_mul", rr[:], zz[:], sinit_nat.tile(0)[0:64, 0:1024])
    V.op("tensor_add", snew[:], nn[:], rr[:])
    _l2dbg(P, C, "snew", snew[0:64, 0:1024], 64, 1024)
    P.barrier()
    gru_es.close()
    snT = P.sbuf("l2snT", [128, 8 * 64], BF16, es=es)
    Tc.wait(V, V.n)
    transpose_64(lambda k: snew[0:64, k * 128:(k + 1) * 128], snT, 8)
    wes2 = ExitStack()
    qkv = P.sbuf("l2qkv", [64, 1536], BF16, es=wes2)
    qkv_es = ExitStack()
    wqkv = load_weight_bf16(P, C, W["L2f_wqkv"], D, 1536, "l2wqkv", es=qkv_es)
    bqkvrow_b = P.sbuf("l2bqkvb", [1, 1536], BF16, es=qkv_es)
    load_row_bf16(P, C, W["L2f_bqkvrow"], 1536, bqkvrow_b[0:1, 0:1536])
    P.tw()
    for j in range(0, 1536, 512):
        ps = singles[(j // 512) % 2]
        P.acquire(ps)
        for k in range(NDT):
            P.mm(ps[0:64, 0:512], snT[:, k * 64:(k + 1) * 64],
                 wqkv.tile(k)[:, j:j + 512], k == 0, False)
        P.mm(ps[0:64, 0:512], onesrow_b[0:1, 0:64],
             bqkvrow_b[0:1, j:j + 512], False, True, inc=True)
        ps.ready = Tc.n
        V.wait(Tc, ps.ready)
        cc = V.op("tensor_copy", qkv[0:64, j:j + 512], ps[0:64, 0:512])
        ps.last_consumer = (V, cc)
    P.barrier()
    qkv_es.close()
    qT = P.sbuf("l2qT", [128, 4 * 64], BF16, es=wes2)
    Tc.wait(V, V.n)
    transpose_64(lambda k: qkv[0:64, k * 128:(k + 1) * 128], qT, 4)
    att = P.sbuf("l2att", [64, 64], F32, es=wes2)
    attb = P.sbuf("l2attb", [64, 64], BF16, es=wes2)
    ps = singles[2]
    P.acquire(ps)
    P.tw()
    for dq in range(2):
        P.mm(ps[0:64, 0:64], qT[:, dq * 64:(dq + 1) * 64],
             qT[:, 128 + dq * 64: 128 + (dq + 1) * 64], dq == 0, dq == 1,
             inc=(dq == 1))
    ps.ready = Tc.n
    mc2 = P.sbuf("l2mc2", [64, 1], F32, es=wes2)
    zc2 = P.sbuf("l2zc2", [64, 1], F32, es=wes2)
    V.wait(Tc, ps.ready)
    V.op("tensor_reduce", mc2[:], ps[0:64, 0:64], AX.X, OP.max)
    cm = V.op("tensor_scalar", mc2[:], mc2[:], -1.0 / 16.0, None, OP.mult)
    S.wait(V, cm)
    ca = S.op("activation", att[:], ps[0:64, 0:64], AF.Exp, bias=mc2[:],
              scale=1.0 / 16.0, accum_out=zc2[:])
    ps.last_consumer = (S, ca)
    V.wait(S, ca)
    V.op("reciprocal", zc2[:], zc2[:])
    V.op("tensor_scalar", attb[:], att[:], zc2[:], None, OP.mult)
    attT = P.sbuf("l2attT", [64, 64], BF16, es=wes2)
    Tc.wait(V, V.n)
    P.acquire(ps)
    P.mm(ps[0:64, 0:64], attb[0:64, 0:64], identb[0:64, 0:64], True, True,
         inc=True)
    ps.ready = Tc.n
    V.wait(Tc, ps.ready)
    cc = V.op("tensor_copy", attT[:], ps[0:64, 0:64])
    ps.last_consumer = (V, cc)
    yv = P.sbuf("l2y", [64, 1024], F32, es=wes2)
    ynb = P.sbuf("l2ynb", [64, 1024], BF16, es=wes2)
    P.acquire(pa)
    Tc.wait(V, V.n)
    for j in range(0, 1024, 512):
        P.mm(pa[0:64, j:j + 512], attT[0:64, 0:64],
             qkv[0:64, 512 + j: 512 + j + 512], True, True,
             inc=(j == 512))
    pa.ready = Tc.n
    V.wait(Tc, pa.ready)
    cy = V.op("tensor_tensor", yv[:], pa[0:64, 0:1024], snew[:], OP.add)
    pa.last_consumer = (V, cy)
    V.op("tensor_reduce", mc2[:], yv[:], AX.X, OP.add)
    cm2 = V.op("tensor_scalar", mc2[:], mc2[:], -1.0 / 1024.0, None, OP.mult)
    V.op("tensor_scalar", yv[:], yv[:], mc2[:], None, OP.add)
    cq = V.n
    S.wait(V, cq)
    csq = S.op("activation", C["scr_sq"].sb[0:64, 0:1024], yv[:], AF.Square,
               accum_out=zc2[:])
    V.wait(S, csq)
    V.op("tensor_scalar", zc2[:], zc2[:], 1.0 / 1024.0, 1e-5, OP.mult, OP.add)
    cz2 = V.n
    S.wait(V, cz2)
    cs2 = S.op("activation", zc2[:], zc2[:], AF.Sqrt)
    V.wait(S, cs2)
    V.op("reciprocal", zc2[:], zc2[:])
    V.op("tensor_scalar", ynb[:], yv[:], zc2[:], None, OP.mult)
    wvp = load_weight_bf16(P, C, W["L2f_wvp"], D, 1024, "wvp", es=wes2)
    lngc = colbuf(P, C, W["L2f_lngc"], 8, "lngc", wes2)
    lnbc = colbuf(P, C, W["L2f_lnbc"], 8, "lnbc", wes2)
    lnbcb = P.sbuf("l2lnbcb", [128, 8], BF16, es=wes2)
    bvprow_b = P.sbuf("l2bvpb", [1, 1024], BF16, es=wes2)
    load_row_bf16(P, C, W["L2f_bvprow"], 1024, bvprow_b[0:1, 0:1024])
    V.op("tensor_copy", lnbcb[:], lnbc)
    brow = P.sbuf("l2brow", [1, 1024], BF16, es=wes2)
    P.tw()
    for j in range(0, 1024, 512):
        ps = singles[(j // 512) % 2]
        P.acquire(ps)
        for k in range(NDT):
            P.mm(ps[0:1, 0:512], lnbcb[:, k:k + 1],
                 wvp.tile(k)[:, j:j + 512], k == 0, False)
        P.mm(ps[0:1, 0:512], onesrow_b[0:1, 0:1], bvprow_b[0:1, j:j + 512],
             False, True, inc=True)
        ps.ready = Tc.n
        V.wait(Tc, ps.ready)
        cc = V.op("tensor_copy", brow[0:1, j:j + 512], ps[0:1, 0:512])
        ps.last_consumer = (V, cc)
    gw = 0
    V.wait(P.G, P.G.n)
    for k in range(NDT):
        gw = V.op("tensor_scalar", wvp.tile(k), wvp.tile(k), lngc[:, k:k + 1],
                  None, OP.mult)
    ynT = P.sbuf("l2ynT", [128, 8 * 64], BF16, es=wes2)
    Tc.wait(V, V.n)
    transpose_64(lambda k: ynb[0:64, k * 128:(k + 1) * 128], ynT, 8)
    sv_t = P.sbuf("l2sv", [64, 1024], BF16, es=wes2)
    sv = sv_t[0:64, 0:1024]
    P.tw()
    for j in range(0, 1024, 512):
        ps = singles[2]
        P.acquire(ps)
        for k in range(NDT):
            P.mm(ps[0:64, 0:512], ynT[:, k * 64:(k + 1) * 64],
                 wvp.tile(k)[:, j:j + 512], k == 0, False)
        P.mm(ps[0:64, 0:512], onesrow_b[0:1, 0:64], brow[0:1, j:j + 512],
             False, True, inc=True)
        ps.ready = Tc.n
        V.wait(Tc, ps.ready)
        cc = V.op("tensor_copy", sv[:, j:j + 512], ps[0:64, 0:512])
        ps.last_consumer = (V, cc)
    _l2dbg(P, C, "sv", sv[:, 0:1024], 64, 1024)
    P.tw()
    for o in range(NDT):
        for c in range(4):
            ps = singles[c % 2]
            P.acquire(ps)
            P.mm(ps[0:128, 0:512], sv[0:64, o * 128:(o + 1) * 128],
                 alphaT[0:64, c * 512:(c + 1) * 512], True, True, inc=True)
            ps.ready = Tc.n
            V.wait(Tc, ps.ready)
            cc = V.op("tensor_copy", hbuf.chunk(o, c * 512, 512),
                      ps[0:128, 0:512])
            ps.last_consumer = (V, cc)
    P.barrier()
    wes2.close()
    wo = load_weight_bf16(P, C, W["L2f_wo"], D, 1024, "l2wo", es=es)
    boc = colbuf(P, C, W["L2f_boc"], 8, "l2boc", es)

    def ep_out(o, c0, w, pb):
        evac_simple(P, C, pb, out_fm.chunk(o, c0, w), bias_col=boc[:, o:o + 1])
    emit_proj_fm(P, C, wo, hbuf, D, ep_out)
    P.barrier()
    es.close()



def emit_l1_ffn(P, C, li, xn, out_fm):
    V, S, Tc, G, SY = P.V, P.S, P.Tc, P.G, P.chains["sync"]
    W = C["W"]
    es = ExitStack()
    pa = C["ps_a"]
    singles = [C["ps1"], C["ps2"], C["ps3"], C["ps4"]]
    identb = C["cst"]["id128_bf"]
    identf = C["cst"]["id128_f32"]
    onesrow_f = C["cst"]["ones_row_f32"]
    onesrow_b = C["cst"]["ones_row_bf16"]
    xn1 = C["s"]
    SY.wait(V, V.n)
    for i in range(NDT):
        P.dma("sync", C["xn_spill"][i * 128:(i + 1) * 128, :], xn.tile(i))
    xns_done = P.dma_counts["sync"]
    g1 = colbuf(P, C, W["L1f_ln1gc"], 8, "ln1g", es)
    b1 = colbuf(P, C, W["L1f_ln1bc"], 8, "ln1b", es)
    emit_layernorm(P, C, xn, xn1, gain_col=lambda i: g1[:, i:i + 1],
                   bias_col=lambda i: b1[:, i:i + 1])
    P.barrier()
    cos2 = P.sbuf("l1cos", [128, 2 * T], BF16, es=es)
    for nm, off in (("L1f_cos2", 0), ("L1f_sin2", T)):
        key = ("x", 0)
        stage_guard(P, C, key)
        t = P.dma("sync", C["xstage"].sb[:, 0:T], W[nm])
        P.dma_wait(V, "sync", t)
        cc = V.op("tensor_copy", cos2[:, off:off + T], C["xstage"].sb[:, 0:T])
        C["stage_consumer"][key] = (V, cc)
    rotT = load_weight_bf16(P, C, W["L1f_rotT"], 128, 128, "rotT", es=es)

    def rope(dst_fm, src_tiles, nt):
        for i in range(nt):
            P.tw()
            P.acquire(pa)
            for j in range(0, T, 512):
                P.mm(pa[:, j:j + 512], rotT.tile(0), src_tiles(i)[:, j:j + 512],
                     True, True, inc=(j == T - 512))
            pa.ready = Tc.n
            V.wait(Tc, pa.ready)
            V.op("tensor_mul", C["scr_sq"].chunk(0, 0, T), src_tiles(i),
                 cos2[:, 0:T])
            cm = V.op("tensor_mul", C["scr_sq"].chunk(1, 0, T), pa[:, 0:T],
                      cos2[:, T:2 * T])
            pa.last_consumer = (V, cm)
            V.op("tensor_add", dst_fm.tile(i), C["scr_sq"].chunk(0, 0, T),
                 C["scr_sq"].chunk(1, 0, T))

    qro = C["g"]
    wes = ExitStack()
    wq = load_weight_bf16(P, C, W["L1f_wq"], D, 1024, "l1wq", es=wes)

    def ep_q(o, c0, w, pb):
        evac_simple(P, C, pb, C["scr_b1"].sb[:, 0:w])
        c = V.op("tensor_copy", qro.chunk(o, c0, w), C["scr_b1"].sb[:, 0:w])
    emit_proj_fm(P, C, wq, xn1, D, ep_q)
    P.barrier()
    rope(qro, lambda i: qro.tile(i), NDT)
    wes.close()
    kro = FM(P, "l1kro", 256, BF16, es=es)
    wes = ExitStack()
    wk = load_weight_bf16(P, C, W["L1f_wk"], D, 256, "l1wk", es=wes)

    def ep_k(o, c0, w, pb):
        evac_simple(P, C, pb, kro.chunk(o, c0, w))
    emit_proj_fm(P, C, wk, xn1, 256, ep_k)
    P.barrier()
    rope(kro, lambda i: kro.tile(i), 2)
    wes.close()
    vext = P.sbuf("l1vext", [128, 16 * 260], BF16, es=es)
    wes = ExitStack()
    wv = load_weight_bf16(P, C, W["L1f_wv"], D, 256, "l1wv", es=wes)
    mset = V.op("memset", vext[:], 1.0)
    P.tw()
    for tt in range(NTT):
        P.acquire(pa)
        for k in range(NDT):
            lhsT = xn1.tile(k)[:, tt * 128:(tt + 1) * 128]
            P.mm(pa[:, 0:256], lhsT, wv.tile(0)[0:128, 0:256] if False else
                 wv.tile(k)[0:128, 0:256], k == 0, k == NDT - 1,
                 inc=(k == NDT - 1))
        pa.ready = Tc.n
        V.wait(Tc, pa.ready)
        src3 = pa.ap[:, 0:256].rearrange("p (g c) -> p g c", g=4)
        dst3 = vext[:, tt * 260:(tt + 1) * 260].rearrange(
            "p (g c) -> p g c", g=4, c=65)[:, :, 0:64]
        cc = V.op("tensor_copy", dst3, src3)
        pa.last_consumer = (V, cc)
    P.barrier()
    wes.close()
    V.wait("dma_sync", xns_done)
    scM = P.sbuf("l1scm", [128, 16 * 512], BF16, es=es)
    rb = P.sbuf("l1rb", [128, 512], BF16, es=es)
    khead = P.sbuf("l1khead", [64, 2048], BF16, es=es)
    qhead = P.sbuf("l1qhead", [64, 512], BF16, es=es)
    negc = C["cst"]["c_negC12"]
    kh_used = 0
    qh_used = 0
    for h in range(16):
        g = h // 4
        ktile = g // 2
        kbase = (g % 2) * 64
        qtile = h // 2
        qbase = (h % 2) * 64
        V.wait(Tc, kh_used)
        ckh = V.op("tensor_copy", khead[0:64, 0:T],
                   kro.tile(ktile)[kbase:kbase + 64, 0:T])
        for sc in range(4):
            s0 = sc * 512
            ntt = 4 * sc + 4
            V.wait(Tc, qh_used)
            cqh = V.op("tensor_copy", qhead[0:64, 0:512],
                       qro.tile(qtile)[qbase:qbase + 64, s0:s0 + 512])
            Tc.wait(V, cqh)
            mask_cnt = G.n
            for tt in range(ntt):
                ps = singles[tt % 3]
                P.acquire(ps)
                P.mm(ps[0:128, 0:512],
                     khead[0:64, tt * 128:(tt + 1) * 128],
                     qhead[0:64, 0:512],
                     True, True, inc=True)
                qh_used = Tc.n
                kh_used = Tc.n
                ps.ready = Tc.n
                S.wait(Tc, ps.ready)
                ca = S.op("activation", scM[:, tt * 512:(tt + 1) * 512],
                          ps[0:128, 0:512], AF.Exp, bias=negc[0:128, 0:1],
                          scale=0.125)
                ps.last_consumer = (S, ca)
                if tt >= 4 * sc:
                    G.wait(S, ca)
                    mask_cnt = G.op("affine_select",
                                    scM[:, tt * 512:(tt + 1) * 512],
                                    scM[:, tt * 512:(tt + 1) * 512],
                                    pattern=[[1, 512]], compare_op=OP.is_ge,
                                    fill=0.0, base=s0 - tt * 128,
                                    channel_multiplier=-1)
            ps1 = singles[3]
            P.acquire(ps1)
            Tc.wait(G, mask_cnt)
            Tc.wait(S, S.n)
            for tt in range(ntt):
                P.mm(ps1[0:65, 0:512],
                     vext[:, tt * 260 + g * 65: tt * 260 + (g + 1) * 65],
                     scM[:, tt * 512:(tt + 1) * 512], tt == 0, tt == ntt - 1,
                     inc=(tt == ntt - 1))
            ps1.ready = Tc.n
            V.wait(Tc, ps1.ready)
            V.op("tensor_copy", C["row_a"].sb[0:1, 0:512], ps1[64:65, 0:512])
            c2 = V.op("reciprocal", C["row_a"].sb[0:1, 0:512],
                      C["row_a"].sb[0:1, 0:512])
            ps2 = singles[2]
            P.acquire(ps2)
            Tc.wait(V, c2)
            P.mm(ps2[0:64, 0:512], onesrow_f[0:1, 0:64],
                 C["row_a"].sb[0:1, 0:512], True, True, inc=True)
            ps2.ready = Tc.n
            V.wait(Tc, ps2.ready)
            c3 = V.op("tensor_copy", rb[0:64, 0:512], ps2[0:64, 0:512])
            ps2.last_consumer = (V, c3)
            c4 = V.op("tensor_mul", xn.tile(qtile)[qbase:qbase + 64, s0:s0 + 512],
                      ps1[0:64, 0:512], rb[0:64, 0:512])
            ps1.last_consumer = (V, c4)
    P.barrier()
    es.close()
    es = ExitStack()
    if CONFIG.get("l2dbg"):
        _l2dbg(P, C, "l1y0", xn.tile(0), 128, 2048)
        _l2dbg(P, C, "l1qro0", qro.tile(0), 128, 2048)
    wes = ExitStack()
    wo = load_weight_bf16(P, C, W["L1f_wo"], D, 1024, "l1wo", es=wes)

    def ep_xmid(o, c0, w, pb):
        key = ("xm", 0)
        stage_guard(P, C, key)
        t = P.dma("sync", C["scr_sq"].chunk(0, 0, w),
                  C["xn_spill"][o * 128:(o + 1) * 128, c0:c0 + w])
        P.dma_wait(V, "sync", t)
        V.wait(Tc, pb.ready)
        c = V.op("tensor_tensor", C["scr_sq"].chunk(0, 0, w), pb[:, 0:w],
                 C["scr_sq"].chunk(0, 0, w), OP.add)
        pb.last_consumer = (V, c)
        c2 = V.op("tensor_copy", xn1.chunk(o, c0, w),
                  C["scr_sq"].chunk(0, 0, w))
        C["stage_consumer"][key] = (V, c2)
    emit_proj_fm(P, C, wo, xn, D, ep_xmid)
    P.barrier()
    wes.close()
    if CONFIG.get("l2dbg"):
        _l2dbg(P, C, "l1xmid0", xn1.tile(0), 128, 2048)
    g2 = colbuf(P, C, W["L1f_ln2gc"], 8, "ln2g", es)
    b2 = colbuf(P, C, W["L1f_ln2bc"], 8, "ln2b", es)
    emit_layernorm(P, C, xn1, xn, gain_col=lambda i: g2[:, i:i + 1],
                   bias_col=lambda i: b2[:, i:i + 1])
    P.barrier()
    xn2 = xn
    if CONFIG.get("l2dbg"):
        _l2dbg(P, C, "l1xn2_0", xn2.tile(0), 128, 2048)
    SY.wait(V, V.n)
    for i in range(NDT):
        P.dma("sync", C["g_spill"][i * 128:(i + 1) * 128, :], xn1.tile(i))
    xmid_done = P.dma_counts["sync"]
    masks = P.sbuf("l1masks", [128, 2 * 160], BF16, es=es)
    awcols = P.sbuf("l1awc", [128, 32], BF16, es=es)
    mask_fm = P.sbuf("l1maskfm", [10, 2 * 2048], BF16, es=es)
    elb = P.sbuf("l1elb", [20, 2048], BF16, es=es)
    gat_es = ExitStack()
    ag = load_weight_bf16(P, C, W["L1f_ag"], D, 10, "l1ag", es=gat_es)
    agl = P.sbuf("l1agl", [10, 2048], F32, es=gat_es)
    P.tw()
    P.acquire(pa)
    for k in range(NDT):
        for j in range(0, T, 512):
            P.mm(pa[0:10, j:j + 512], ag.tile(k)[:, 0:10],
                 xn2.chunk(k, j, 512), k == 0, k == NDT - 1,
                 inc=(k == NDT - 1 and j == T - 512))
    pa.ready = Tc.n
    V.wait(Tc, pa.ready)
    cagl = V.op("tensor_copy", agl[0:10, 0:T], pa[0:10, 0:T])
    pa.last_consumer = (V, cagl)
    agtm = P.sbuf("l1agtm", [128, 16 * 10], F32, es=gat_es)
    Tc.wait(V, cagl)
    for tt in range(NTT):
        ps = singles[tt % 2]
        P.acquire(ps)
        P.mm(ps[0:128, 0:10], agl[0:10, tt * 128:(tt + 1) * 128],
             identf[0:10, 0:10], True, True, inc=True)
        ps.ready = Tc.n
        V.wait(Tc, ps.ready)
        cc = V.op("tensor_copy", agtm[:, tt * 10:(tt + 1) * 10],
                  ps[0:128, 0:10])
        ps.last_consumer = (V, cc)
    probs = P.sbuf("l1probs", [128, 10], F32, es=gat_es)
    wrk = P.sbuf("l1wrk", [128, 10], F32, es=gat_es)
    mcol = P.sbuf("l1m", [128, 1], F32, es=gat_es)
    zcol = P.sbuf("l1z", [128, 1], F32, es=gat_es)
    m2col = P.sbuf("l1m2", [128, 1], F32, es=gat_es)
    for tt in range(NTT):
        tv = agtm[:, tt * 10:(tt + 1) * 10]
        V.op("tensor_reduce", mcol[:], tv, AX.X, OP.max, negate=True)
        S.wait(V, V.n)
        ce = S.op("activation", probs[:], tv, AF.Exp, bias=mcol[:],
                  accum_out=zcol[:])
        V.wait(S, ce)
        V.op("reciprocal", zcol[:], zcol[:])
        V.op("tensor_scalar", probs[:], probs[:], zcol[:], None, OP.mult)
        V.op("tensor_reduce", mcol[:], probs[:], AX.X, OP.max)
        V.op("tensor_scalar", masks[:, tt * 10:(tt + 1) * 10], probs[:],
             mcol[:], None, OP.is_ge)
        V.op("tensor_copy", wrk[:], probs[:])
        V.op("scalar_tensor_tensor", wrk[:], masks[:, tt * 10:(tt + 1) * 10],
             -9.0, wrk[:], OP.mult, OP.add)
        V.op("tensor_reduce", m2col[:], wrk[:], AX.X, OP.max)
        V.op("tensor_scalar", masks[:, 160 + tt * 10:160 + (tt + 1) * 10],
             wrk[:], m2col[:], None, OP.is_ge)
        V.op("tensor_add", zcol[:], mcol[:], m2col[:])
        V.op("tensor_scalar", zcol[:], zcol[:], 1e-6, None, OP.add)
        V.op("reciprocal", zcol[:], zcol[:])
        V.op("tensor_tensor", awcols[:, tt:tt + 1], mcol[:], zcol[:], OP.mult)
        V.op("tensor_tensor", awcols[:, 16 + tt:16 + tt + 1], m2col[:],
             zcol[:], OP.mult)
    Tc.wait(V, V.n)
    for k in range(2):
        for tt in range(NTT):
            ps = singles[tt % 2]
            P.acquire(ps)
            P.mm(ps[0:10, 0:128], masks[:, k * 160 + tt * 10:
                                        k * 160 + (tt + 1) * 10],
                 identb, True, True, inc=True)
            ps.ready = Tc.n
            V.wait(Tc, ps.ready)
            cc = V.op("tensor_copy",
                      mask_fm[0:10, k * 2048 + tt * 128:
                              k * 2048 + (tt + 1) * 128], ps[0:10, 0:128])
            ps.last_consumer = (V, cc)
    P.barrier()
    gat_es.close()
    el_es = ExitStack()
    eg = load_weight_bf16(P, C, W["L1f_eg"], D, 20, "l1eg", es=el_es)
    P.tw()
    P.acquire(pa)
    for k in range(NDT):
        for j in range(0, T, 512):
            P.mm(pa[0:20, j:j + 512], eg.tile(k)[:, 0:20],
                 xn2.chunk(k, j, 512), k == 0, k == NDT - 1,
                 inc=(k == NDT - 1 and j == T - 512))
    pa.ready = Tc.n
    V.wait(Tc, pa.ready)
    cel = V.op("tensor_copy", elb[0:20, 0:T], pa[0:20, 0:T])
    pa.last_consumer = (V, cel)
    P.barrier()
    el_es.close()
    ncores = CONFIG.get("n_cores", 8)
    sel = P.sbuf("l1sel", [10, 2], BF16, es=es)
    if C.get("l1_probe"):
        SY.wait(V, V.n)
        P.dma("sync", C["l1_sel_out"][0:10, 0:1], mask_fm[0:10, 2047:2048],
              allow_slow_non_contiguous=True)
        P.dma("sync", C["l1_sel_out"][0:10, 1:2], mask_fm[0:10, 4095:4096],
              allow_slow_non_contiguous=True)
        P.barrier()
        es.close()
        return
    if "l1_sel_in" in C:
        t3 = P.dma("sync", sel[:], C["l1_sel_in"])
    else:
        SY.wait(V, V.n)
        t1 = P.dma("sync", C["cc_in"][0:10, 0:1], mask_fm[0:10, 2047:2048],
                   allow_slow_non_contiguous=True)
        t2 = P.dma("sync", C["cc_in"][0:10, 1:2], mask_fm[0:10, 4095:4096],
                   allow_slow_non_contiguous=True)
        P.dma_wait(G, "sync", t2)
        ccin_ap = C["cc_in"]
        ccout_ap = C["cc_out"]
        semname = "dma_gpsimd"
        if ncores == 1:
            cc_tgt = P.dma("gpsimd", ccout_ap[0:10, :], ccin_ap)
        else:
            P.dma_counts["gpsimd"] += 16
            cc_tgt = P.dma_counts["gpsimd"]

            def fcc(eng, ccin_ap=ccin_ap, ccout_ap=ccout_ap, ncores=ncores):
                eng.collective_compute(
                    "AllGather", OP.bypass, [list(range(ncores))],
                    [ccin_ap], [ccout_ap]).then_inc(P.sems[semname], 16)
            G.ops.append(fcc)
        P.dma_wait(SY, "gpsimd", cc_tgt)
        t3 = P.dma("sync", sel[:], C["cc_out"][0:10, :])
    iotac = P.sbuf("l1iota", [128, 1], F32, es=es)
    G.op("iota", iotac[:], pattern=[[0, 1]], base=0, channel_multiplier=1,
         allow_small_or_imprecise_dtypes=True)
    acolk = P.sbuf("l1acol", [128, 2], F32, es=es)
    o20 = P.sbuf("l1o20", [20, 4], BF16, es=es)
    a_sb = P.sbuf("l1asb", [1, 2], F32, es=es)
    iota10 = C["cst"]["iota10col"]
    exp1020 = C["cst"]["exp1020"]
    P.dma_wait(Tc, "sync", t3)
    for k in range(2):
        ps = singles[0]
        P.acquire(ps)
        P.mm(ps[0:1, 0:1], sel[0:10, k:k + 1], iota10[0:10, 0:1], True, True,
             inc=True)
        ps.ready = Tc.n
        V.wait(Tc, ps.ready)
        cc = V.op("tensor_copy", a_sb[0:1, k:k + 1], ps[0:1, 0:1])
        ps.last_consumer = (V, cc)
        for j in range(2):
            ps = singles[1]
            P.acquire(ps)
            P.mm(ps[0:20, 0:1], exp1020[0:10, j * 20:(j + 1) * 20],
                 sel[0:10, k:k + 1], True, True, inc=True)
            ps.ready = Tc.n
            V.wait(Tc, ps.ready)
            cc = V.op("tensor_copy", o20[0:20, k * 2 + j:k * 2 + j + 1],
                      ps[0:20, 0:1])
            ps.last_consumer = (V, cc)
    Tc.wait(V, V.n)
    for k in range(2):
        ps = singles[0]
        P.acquire(ps)
        P.mm(ps[0:128, 0:1], onesrow_f, a_sb[0:1, k:k + 1], True, True,
             inc=True)
        ps.ready = Tc.n
        V.wait(Tc, ps.ready)
        cc = V.op("tensor_copy", acolk[:, k:k + 1], ps[0:128, 0:1])
        ps.last_consumer = (V, cc)
    role = load_weight_bf16(P, C, W["L1f_role"], 10, 1024, "l1role", es=es)
    wslotA = FM(P, "l1wA", D, BF16, tokens=1024, es=es)
    wslotB = FM(P, "l1wB", D, BF16, tokens=1024, es=es)
    w1v = FMView(wslotA.sb, D, 1024)
    w3v = FMView(wslotB.sb, D, 1024)
    w2v = FMView(wslotA.sb, 2048, 512)
    hv = FMView(xn1.sb, 2048, 1024)
    awbk = P.sbuf("l1awbk", [128, 2048], BF16, es=es)
    aewb = P.sbuf("l1aewb", [128, 2048], BF16, es=es)
    fbuf = P.sbuf("l1fbuf", [128, 1024], BF16, es=es)
    idxf = P.sbuf("l1idxf", [128, 1], F32, es=es)
    idxi = P.sbuf("l1idxi", [128, 1], I32, es=es)
    V.wait("dma_sync", xmid_done)
    C["l1_gather_tgt"] = [0]
    slotA_used = [0]
    slotB_used = [0]
    w1flat = W["L1f_W1"].rearrange("e d f -> (e d) f")
    w3flat = W["L1f_W3"].rearrange("e d f -> (e d) f")
    w2flat = W["L1f_W2"].rearrange("e f c -> (e f) c")

    def gather_w(flat_ap, kk, joff_rows, estride, nchunks, ncols, col0,
                 dstv, used):
        G.wait(Tc, used[0])
        C_full = flat_ap.shape[-1]
        cf = C_full // ncols
        for r in range(nchunks):
            P.dma_wait(V, "gpsimd", C["l1_gather_tgt"][0])
            cv1 = V.op("tensor_scalar", idxf[:], acolk[:, kk:kk + 1],
                       float(estride), float(joff_rows + r * 128), OP.mult,
                       OP.add)
            cv3 = V.op("tensor_tensor", idxf[:], idxf[:], iotac[:], OP.add)
            cv3b = V.op("tensor_scalar", idxf[:], idxf[:], float(cf),
                        float(col0 // ncols), OP.mult, OP.add)
            cv4 = V.op("tensor_copy", idxi[:], idxf[:])
            G.wait(V, cv4)
            key = ("wstage", 0)
            stage_guard(P, C, key, q="gpsimd")
            P.dma_counts["gpsimd"] += 16
            gt = P.dma_counts["gpsimd"]
            stg = C["wstage"].tile(0)[0:128, 0:ncols]

            def fg(eng, stg=stg, flat_ap=flat_ap, ncols=ncols):
                eng.indirect_dma_start(
                    out=stg, out_offset=None, in_=flat_ap[:, 0:ncols],
                    in_offset=IOA(ap=idxi[:], axis=0)).then_inc(
                        P.sems["dma_gpsimd"], 16)
            G.ops.append(fg)
            C["l1_gather_tgt"][0] = gt
            G.wait("dma_gpsimd", gt)
            cg = G.op("tensor_copy", dstv.tile(r)[:, 0:ncols], stg)
            C["stage_consumer"][key] = (G, cg)

    first = [True]
    for kk in range(2):
        Tc.wait(V, V.n)
        for tt in range(NTT):
            ps = singles[tt % 2]
            P.acquire(ps)
            P.mm(ps[0:1, 0:128], awcols[:, kk * 16 + tt:kk * 16 + tt + 1],
                 identb, True, True, inc=True)
            ps.ready = Tc.n
            V.wait(Tc, ps.ready)
            cc = V.op("tensor_copy",
                      C["scr_b2"].sb[0:1, tt * 128:(tt + 1) * 128],
                      ps[0:1, 0:128])
            ps.last_consumer = (V, cc)
        Tc.wait(V, V.n)
        for j4 in range(0, T, 512):
            ps = singles[2 + (j4 // 512) % 2]
            P.acquire(ps)
            P.mm(ps[0:128, 0:512], onesrow_b,
                 C["scr_b2"].sb[0:1, j4:j4 + 512], True, True, inc=True)
            ps.ready = Tc.n
            V.wait(Tc, ps.ready)
            cc = V.op("tensor_copy", awbk[:, j4:j4 + 512], ps[0:128, 0:512])
            ps.last_consumer = (V, cc)
        for j in range(2):
            P.tw()
            for c in range(4):
                ps = singles[c % 2]
                P.acquire(ps)
                P.mm(ps[0:1, 0:512], o20[0:20, kk * 2 + j:kk * 2 + j + 1],
                     elb[0:20, c * 512:(c + 1) * 512], True, True, inc=True)
                ps.ready = Tc.n
                V.wait(Tc, ps.ready)
                cc = V.op("tensor_copy",
                          C["row_a"].sb[0:1, c * 512:(c + 1) * 512],
                          ps[0:1, 0:512])
                ps.last_consumer = (V, cc)
            if j == 0:
                V.op("tensor_copy", C["row_b"].sb[0:1, 0:T],
                     C["row_a"].sb[0:1, 0:T])
        V.op("tensor_sub", C["row_b"].sb[0:1, 0:T], C["row_b"].sb[0:1, 0:T],
             C["row_a"].sb[0:1, 0:T])
        cs = V.n
        S.wait(V, cs)
        csg = S.op("activation", C["row_b"].sb[0:1, 0:T],
                   C["row_b"].sb[0:1, 0:T], AF.Sigmoid)
        V.wait(S, csg)
        V.op("tensor_scalar", C["row_a"].sb[0:1, 0:T],
             C["row_b"].sb[0:1, 0:T], -1.0, 1.0, OP.mult, OP.add)
        for j in range(2):
            rowsrc = C["row_b"] if j == 0 else C["row_a"]
            Tc.wait(V, V.n)
            for c in range(4):
                ps = singles[2 + c % 2]
                P.acquire(ps)
                P.mm(ps[0:128, 0:512], onesrow_f,
                     rowsrc.sb[0:1, c * 512:(c + 1) * 512], True, True,
                     inc=True)
                ps.ready = Tc.n
                V.wait(Tc, ps.ready)
                cc = V.op("tensor_copy",
                          aewb[:, c * 512:(c + 1) * 512],
                          ps[0:128, 0:512])
                ps.last_consumer = (V, cc)
            for th in range(2):
                t0 = th * 1024
                for half in range(2):
                    gather_w(w1flat, kk, j * 1024, 2048, 8, 1024,
                             half * 1024, w1v, slotA_used)
                    gather_w(w3flat, kk, j * 1024, 2048, 8, 1024,
                             half * 1024, w3v, slotB_used)
                    P.tw()
                    for f8 in range(8):
                        fglob = half * 8 + f8
                        P.acquire(pa)
                        for k in range(NDT):
                            l1 = w1v.tile(k)[:, f8 * 128:(f8 + 1) * 128]
                            l3 = w3v.tile(k)[:, f8 * 128:(f8 + 1) * 128]
                            for jc in range(2):
                                P.mm(pa[:, jc * 512:(jc + 1) * 512], l1,
                                     xn2.chunk(k, t0 + jc * 512, 512),
                                     k == 0, k == NDT - 1)
                                P.mm(pa[:, 1024 + jc * 512:1024 + (jc + 1) * 512],
                                     l3, xn2.chunk(k, t0 + jc * 512, 512),
                                     k == 0, k == NDT - 1,
                                     inc=(k == NDT - 1 and jc == 1))
                        pa.ready = Tc.n
                        S.wait(Tc, pa.ready)
                        cs1 = S.op("activation", C["scr_sq"].chunk(0, 0, 1024),
                                   pa[:, 0:1024], AF.Silu)
                        V.wait(S, cs1)
                        ch1 = V.op("tensor_mul", hv.tile(fglob)[:, 0:1024],
                                   C["scr_sq"].chunk(0, 0, 1024),
                                   pa[:, 1024:2048])
                        pa.last_consumer = (V, ch1)
                    slotA_used[0] = Tc.n
                    slotB_used[0] = Tc.n
                for ch in range(2):
                    gather_w(w2flat, kk, j * 2048, 4096, 16, 512,
                             ch * 512, w2v, slotA_used)
                    P.tw()
                    for c4 in range(4):
                        c = ch * 4 + c4
                        P.acquire(pa)
                        for f in range(16):
                            l2 = w2v.tile(f)[:, c4 * 128:(c4 + 1) * 128]
                            for jc in range(2):
                                P.mm(pa[:, jc * 512:(jc + 1) * 512], l2,
                                     hv.tile(f)[:, jc * 512:(jc + 1) * 512],
                                     f == 0, f == 15,
                                     inc=(f == 15 and jc == 1))
                        pa.ready = Tc.n
                        ps = singles[2]
                        P.acquire(ps)
                        psb2 = singles[3]
                        P.acquire(psb2)
                        P.mm(ps[0:128, 0:512],
                             role.tile(0)[0:10, c * 128:(c + 1) * 128],
                             mask_fm[0:10, kk * 2048 + t0:kk * 2048 + t0 + 512],
                             True, True, inc=True)
                        ps.ready = Tc.n
                        P.mm(psb2[0:128, 0:512],
                             role.tile(0)[0:10, c * 128:(c + 1) * 128],
                             mask_fm[0:10, kk * 2048 + t0 + 512:
                                     kk * 2048 + t0 + 1024],
                             True, True, inc=True)
                        psb2.ready = Tc.n
                        V.wait(Tc, psb2.ready)
                        V.op("tensor_scalar", fbuf[:, 0:512], ps[0:128, 0:512],
                             0.1, 1.0, OP.mult, OP.add)
                        cf = V.op("tensor_scalar", fbuf[:, 512:1024],
                                  psb2[0:128, 0:512], 0.1, 1.0, OP.mult,
                                  OP.add)
                        ps.last_consumer = (V, cf)
                        psb2.last_consumer = (V, cf)
                        V.op("tensor_mul", fbuf[:, 0:1024], fbuf[:, 0:1024],
                             awbk[:, t0:t0 + 1024])
                        V.op("tensor_mul", fbuf[:, 0:1024], fbuf[:, 0:1024],
                             aewb[:, t0:t0 + 1024])
                        cw = V.op("tensor_mul", C["scr_sq"].chunk(1, 0, 1024),
                                  pa[:, 0:1024], fbuf[:, 0:1024])
                        pa.last_consumer = (V, cw)
                        if first[0] and kk == 0 and j == 0:
                            V.op("tensor_copy", out_fm.chunk(c, t0, 1024),
                                 C["scr_sq"].chunk(1, 0, 1024))
                        else:
                            V.op("tensor_add", out_fm.chunk(c, t0, 1024),
                                 out_fm.chunk(c, t0, 1024),
                                 C["scr_sq"].chunk(1, 0, 1024))
                    slotA_used[0] = Tc.n
        first[0] = False
    P.barrier()
    if CONFIG.get("l2dbg"):
        _l2dbg(P, C, "l1acc0", out_fm.tile(0), 128, 2048)
        _l2dbg(P, C, "l1sel", sel[0:10, 0:2], 10, 2)
        _l2dbg(P, C, "l1acolk", acolk[0:128, 0:2], 128, 2)
        _l2dbg(P, C, "l1awbk", awbk[0:128, 0:2048], 128, 2048)
        _l2dbg(P, C, "l1aewb", aewb[0:128, 0:2048], 128, 2048)
        _l2dbg(P, C, "l1o20", o20[0:20, 0:4], 20, 4)
        _l2dbg(P, C, "l1h0", hv.tile(0), 128, 1024)
        _l2dbg(P, C, "l1w1v0", w1v.tile(0)[:, 0:1024], 128, 1024)
        _l2dbg(P, C, "l1maskfm", mask_fm[0:10, 0:2048], 10, 2048)
        _l2dbg(P, C, "l1maskfm2", mask_fm[0:10, 2048:4096], 10, 2048)
    for i in range(NDT):
        key = ("xm", 0)
        stage_guard(P, C, key)
        t = P.dma("sync", C["scr_sq"].chunk(0, 0, T),
                  C["g_spill"][i * 128:(i + 1) * 128, :])
        P.dma_wait(V, "sync", t)
        cc = V.op("tensor_add", out_fm.tile(i), out_fm.tile(i),
                  C["scr_sq"].chunk(0, 0, T))
        C["stage_consumer"][key] = (V, cc)
    P.barrier()
    for i in range(NDT):
        P.dma("sync", xn.tile(i), C["xn_spill"][i * 128:(i + 1) * 128, :])
    P.barrier()
    es.close()



def emit_layer(P, C, li, x_in_dram, x_out_dram):
    V, S = P.V, P.S
    xn, gbuf, sbuf_ = C["xn"], C["g"], C["s"]
    emit_layernorm(P, C, None, xn, src_dram=x_in_dram)
    P.barrier()
    FFN_EMITTERS[li](P, C, li, xn, gbuf)
    P.barrier()
    for i in range(NDT):
        S.op("activation", gbuf.tile(i), gbuf.tile(i), AF.Silu)
    P.barrier()
    if li in FUSED_ATTN:
        ATTN_EMITTERS[li](P, C, li, xn, sbuf_, x_in_dram, x_out_dram)
        P.barrier()
        return
    ATTN_EMITTERS[li](P, C, li, xn, sbuf_)
    P.barrier()
    for nm in CONFIG.get("dump", []):
        fm = C[nm]
        dap = C["dump_aps"][nm]
        for i in range(fm.nt):
            c = P.V.op("tensor_copy", C["xostage"].tile(0)[0:fm.p, :],
                       fm.tile(i))
            P.chains["sync"].wait(P.V, c)
            st = P.dma("sync", dap[i * 128:i * 128 + fm.p, :],
                       C["xostage"].tile(0)[0:fm.p, :])
            P.dma_wait(P.V, "sync", st)
    for i in range(NDT):
        key = ("x", 0)
        stage_guard(P, C, key)
        tgt = P.dma("sync", C["xstage"].tile(0),
                    x_in_dram[i * 128:(i + 1) * 128, :])
        tm = V.op("tensor_mul", C["scr_sq"].chunk(0, 0, T), sbuf_.tile(i),
                  gbuf.tile(i))
        P.dma_wait(V, "sync", tgt)
        prev_store = C["xo_store"].get(0)
        if prev_store is not None:
            P.dma_wait(V, "sync", prev_store)
        ca = V.op("tensor_tensor", C["xostage"].tile(0),
                  C["xstage"].tile(0), C["scr_sq"].chunk(0, 0, T), OP.add)
        C["stage_consumer"][key] = (V, ca)
        P.chains["sync"].wait(V, ca)
        st = P.dma("sync", x_out_dram[i * 128:(i + 1) * 128, :],
                   C["xostage"].tile(0))
        C["xo_store"][0] = st
    P.barrier()


def stub_ffn(P, C, li, xn, out_fm):
    for i in range(NDT):
        P.V.op("tensor_copy", out_fm.tile(i), xn.tile(i))


def stub_attn(P, C, li, xn, out_fm):
    for i in range(NDT):
        P.V.op("tensor_copy", out_fm.tile(i), xn.tile(i))


FFN_EMITTERS = {0: emit_l0_ffn, 1: emit_l1_ffn, 2: emit_l2_ffn, 3: emit_l3_ffn}
ATTN_EMITTERS = {0: emit_l0_attn, 1: emit_mamba, 2: emit_mamba, 3: emit_l3_attn}
FUSED_ATTN = {0, 3}



def make_consts():
    c = {}
    c["col_inv1024"] = np.full((128, 1), 1.0 / 1024, BF)
    c["col_ones16"] = np.ones((16, 1), BF)
    c["col_ones128"] = np.ones((128, 1), BF)
    c["ones_row_f32"] = np.ones((1, 128), np.float32)
    c["ones_row_bf16"] = np.ones((1, 128), BF)
    c["c_eps_ln"] = np.full((1, 1), 1e-5, np.float32)
    c["c_one"] = np.ones((128, 1), np.float32)
    c["c_negC12"] = np.full((128, 1), -12.0, np.float32)
    c["c_negC4"] = np.full((128, 1), -4.0, np.float32)
    c["eyerep8"] = np.repeat(np.eye(8, dtype=np.float32), 128,
                             axis=1).astype(BF)
    c["iota10col"] = np.arange(10, dtype=np.float32).reshape(10, 1).astype(BF)
    e10 = np.zeros((10, 40), np.float32)
    for j in range(2):
        for a in range(10):
            e10[a, j * 20 + 2 * a + j] = 1.0
    c["exp1020"] = e10.astype(BF)
    c["id128_bf"] = np.eye(128, dtype=BF)
    c["id128_f32"] = np.eye(128, dtype=np.float32)
    c["eyerep31"] = np.repeat(np.eye(31, dtype=np.float32), 64,
                              axis=1).astype(BF)
    return c


def colbuf(P, C, dram_ap, cols, name, es, p=128):
    t = P.sbuf("col_" + name, [p, cols], F32, es=es)
    tgt = P.dma("sync", t[:], dram_ap)
    P.dma_wait(P.V, "sync", tgt)
    P.dma_wait(P.S, "sync", tgt)
    return t[:]


def declare_consts(nc, consts):
    aps = {}
    for k, v in consts.items():
        dt = BF16 if v.dtype == BF else F32
        aps[k] = nc.dram_tensor("c_" + k, list(v.shape), dt,
                                kind="ExternalInput").ap()
    return aps


def load_consts(P, C):
    cst = {}
    for k, ap in C["cst_dram"].items():
        t = P.sbuf("cst_" + k, list(ap.shape), ap.dtype)
        tgt = P.dma("sync", t[:], ap)
        cst[k] = t[:]
    C["cst"] = cst
    for e in ("tensor", "vector", "scalar", "gpsimd"):
        P.dma_wait(P.chains[e], "sync", P.dma_counts["sync"])


def prep_layer_weights(li, inputs, out):
    f32 = np.float32
    cc = np.ascontiguousarray
    if li in (1, 2):
        pfx = f"L{li}a_"
        if li == 1:
            W1 = inputs["L1_attn_W1"].T
            W2 = inputs["L1_attn_W2"].T
            W3 = inputs["L1_attn_W3"].T
            b1, b2, b3 = (inputs["L1_attn_b1"], inputs["L1_attn_b2"],
                          inputs["L1_attn_b3"])
        else:
            W1 = inputs["L2_attn_W1"]
            W2 = inputs["L2_attn_W2"]
            W3 = inputs["L2_attn_W3"]
            b1, b2, b3 = (inputs["L2_attn_b1"], inputs["L2_attn_b2"],
                          inputs["L2_attn_b3"])
        w23 = np.concatenate([W2, W3], axis=1).astype(f32)
        b23 = np.stack([b2, b3], axis=1).astype(f32)
        out[pfx + "w1"] = cc(W1, dtype=f32)
        out[pfx + "w23"] = w23
        out[pfx + "b1c"] = cc(np.asarray(b1, f32).reshape(8, 128).T)
        out[pfx + "b23c"] = b23


def prep_l0_weights(inputs, out):
    f32 = np.float32
    cc = np.ascontiguousarray
    out["L0a_wq"] = cc(inputs["L0_attn_Wq"], dtype=f32)
    out["L0a_wk"] = cc(inputs["L0_attn_Wk"], dtype=f32)
    out["L0a_wv"] = cc(inputs["L0_attn_Wv"], dtype=f32)
    out["L0a_wo"] = cc(inputs["L0_attn_Wo"], dtype=f32)
    out["L0a_bqc"] = cc(np.asarray(inputs["L0_attn_bq"], f32).reshape(16, 128).T)
    out["L0a_bkc"] = cc(np.asarray(inputs["L0_attn_bk"], f32).reshape(16, 128).T)
    out["L0a_bv"] = cc(np.asarray(inputs["L0_attn_bv"], f32).reshape(1, 2048))
    out["L0a_boc"] = cc(np.asarray(inputs["L0_attn_bo"], f32).reshape(8, 128).T)
    out["L0f_wqT"] = cc(np.asarray(inputs["L0_ffn_Wq"], f32).T)
    out["L0f_wkT"] = cc(np.asarray(inputs["L0_ffn_Wk"], f32).T)
    out["L0f_bqc"] = cc(np.asarray(inputs["L0_ffn_bq"], f32).reshape(8, 128).T)
    out["L0f_bkc"] = cc(np.asarray(inputs["L0_ffn_bk"], f32).reshape(8, 128).T)
    out["L0f_eWqT"] = cc(np.asarray(inputs["L0_ffn_eWq"], f32).transpose(0, 2, 1))
    out["L0f_eWkT"] = cc(np.asarray(inputs["L0_ffn_eWk"], f32).transpose(0, 2, 1))
    out["L0f_ebqT"] = cc(np.asarray(inputs["L0_ffn_ebq"], f32).T)
    out["L0f_ebkT"] = cc(np.asarray(inputs["L0_ffn_ebk"], f32).T)
    out["L0f_Wexp"] = cc(inputs["L0_ffn_Wexp"], dtype=f32)
    out["L0f_bexp"] = cc(inputs["L0_ffn_bexp"], dtype=f32)


def prep_l1f_weights(inputs, out):
    f32 = np.float32
    cc = np.ascontiguousarray
    out["L1f_ln1gc"] = cc(np.asarray(inputs["L1_ffn_ln1_g"], f32).reshape(8, 128).T)
    out["L1f_ln1bc"] = cc(np.asarray(inputs["L1_ffn_ln1_b"], f32).reshape(8, 128).T)
    out["L1f_ln2gc"] = cc(np.asarray(inputs["L1_ffn_ln2_g"], f32).reshape(8, 128).T)
    out["L1f_ln2bc"] = cc(np.asarray(inputs["L1_ffn_ln2_b"], f32).reshape(8, 128).T)
    out["L1f_wq"] = cc(inputs["L1_ffn_wq"], dtype=f32)
    out["L1f_wk"] = cc(inputs["L1_ffn_wk"], dtype=f32)
    out["L1f_wv"] = cc(inputs["L1_ffn_wv"], dtype=f32)
    out["L1f_wo"] = cc(inputs["L1_ffn_wo"], dtype=f32)
    out["L1f_ag"] = cc(inputs["L1_ffn_agent_gate"], dtype=f32)
    out["L1f_eg"] = cc(inputs["L1_ffn_expert_gate"], dtype=f32)
    out["L1f_role"] = cc(inputs["L1_ffn_role_emb"], dtype=f32)
    out["L1f_W1"] = cc(inputs["L1_ffn_W1"], dtype=f32)
    out["L1f_W2"] = cc(inputs["L1_ffn_W2"], dtype=f32)
    out["L1f_W3"] = cc(inputs["L1_ffn_W3"], dtype=f32)
    HD = 64
    inv = 1.0 / (10000.0 ** (np.arange(0, HD, 2, dtype=f32) / HD))
    freqs = np.arange(T, dtype=f32)[:, None] * inv[None, :]
    emb = np.concatenate([freqs, freqs], axis=-1)
    cosT = np.cos(emb).T.astype(f32)
    sinT = np.sin(emb).T.astype(f32)
    out["L1f_cos2"] = cc(np.concatenate([cosT, cosT], 0))
    out["L1f_sin2"] = cc(np.concatenate([sinT, sinT], 0))
    R = np.zeros((128, 128), f32)
    for base in (0, 64):
        for r in range(32):
            R[base + r, base + r + 32] = -1.0
            R[base + 32 + r, base + r] = 1.0
    out["L1f_rotT"] = cc(R.T)


def prep_l2f_weights(inputs, out):
    f32 = np.float32
    cc = np.ascontiguousarray
    w1 = np.asarray(inputs["L2_ffn_w1"], f32)
    out["L2f_w1a"] = cc(w1[0:1024])
    out["L2f_w1b"] = cc(w1[1024:2048])
    out["L2f_b1row"] = cc(np.asarray(inputs["L2_ffn_b1"], f32).reshape(1, 512))
    out["L2f_w2"] = cc(inputs["L2_ffn_w2"], dtype=f32)
    out["L2f_b2c"] = cc(np.asarray(inputs["L2_ffn_b2"], f32).reshape(64, 1))
    out["L2f_tau"] = cc(np.asarray(inputs["L2_ffn_tau"], f32).reshape(1, 1))
    out["L2f_sinit"] = cc(inputs["L2_ffn_slot_init"], dtype=f32)
    out["L2f_sinitT"] = cc(np.asarray(inputs["L2_ffn_slot_init"], f32).T)
    out["L2f_wihT"] = cc(np.asarray(inputs["L2_ffn_Wih"], f32).T)
    out["L2f_whhT"] = cc(np.asarray(inputs["L2_ffn_Whh"], f32).T)
    out["L2f_bihrow"] = cc(np.asarray(inputs["L2_ffn_bih"], f32).reshape(1, 3072))
    out["L2f_bhhrow"] = cc(np.asarray(inputs["L2_ffn_bhh"], f32).reshape(1, 3072))
    out["L2f_wqkv"] = cc(np.concatenate(
        [inputs["L2_ffn_wq"], inputs["L2_ffn_wk"], inputs["L2_ffn_wv"]],
        axis=1).astype(f32))
    out["L2f_bqkvrow"] = cc(np.concatenate(
        [inputs["L2_ffn_bq"], inputs["L2_ffn_bk"], inputs["L2_ffn_bv"]]
        ).astype(f32).reshape(1, 1536))
    out["L2f_wvp"] = cc(inputs["L2_ffn_wvp"], dtype=f32)
    out["L2f_bvprow"] = cc(np.asarray(inputs["L2_ffn_bvp"], f32).reshape(1, 1024))
    out["L2f_lngc"] = cc(np.asarray(inputs["L2_ffn_ln_g"], f32).reshape(8, 128).T)
    out["L2f_lnbc"] = cc(np.asarray(inputs["L2_ffn_ln_b"], f32).reshape(8, 128).T)
    out["L2f_wo"] = cc(inputs["L2_ffn_wo"], dtype=f32)
    out["L2f_boc"] = cc(np.asarray(inputs["L2_ffn_bo"], f32).reshape(8, 128).T)


def prep_l3_weights(inputs, out):
    f32 = np.float32
    cc = np.ascontiguousarray
    out["L3t_wqkv"] = cc(inputs["L3_attn_w_qkv"], dtype=f32)
    out["L3t_wout"] = cc(inputs["L3_attn_w_out"], dtype=f32)
    out["L3f_ws1"] = cc(inputs["L3_ffn_ws1"], dtype=f32)
    out["L3f_ws2"] = cc(inputs["L3_ffn_ws2"], dtype=f32)
    out["L3f_bs1c"] = cc(np.asarray(inputs["L3_ffn_bs1"], f32).reshape(8, 128).T)
    out["L3f_bs2c"] = cc(np.asarray(inputs["L3_ffn_bs2"], f32).reshape(8, 128).T)
    out["L3f_wr"] = cc(inputs["L3_ffn_Wr"], dtype=f32)
    out["L3f_brc"] = cc(np.asarray(inputs["L3_ffn_br"], f32).reshape(31, 1))
    out["L3f_We1"] = cc(inputs["L3_ffn_We1"], dtype=f32)
    out["L3f_We2"] = cc(inputs["L3_ffn_We2"], dtype=f32)
    out["L3f_Be1c"] = cc(np.asarray(inputs["L3_ffn_Be1"], f32)
                         .reshape(31, 8, 128).transpose(0, 2, 1))
    out["L3f_Be2"] = cc(inputs["L3_ffn_Be2"], dtype=f32)


def declare_weights(nc, wprep):
    aps = {}
    for k, v in wprep.items():
        dt = BF16 if v.dtype == BF else F32
        aps[k] = nc.dram_tensor(k, list(v.shape), dt, kind="ExternalInput").ap()
    return aps


def build_group(layers, wprep, consts):
    nc = bass.Bass("TRN2", target_bir_lowering=False)
    x_in = nc.dram_tensor("x_in", [D, T], F32, kind="ExternalInput").ap()
    x_out = nc.dram_tensor("x_out", [D, T], F32, kind="ExternalOutput").ap()
    inter = [nc.dram_tensor(f"x_mid{i}", [D, T], F32).ap()
             for i in range(len(layers) - 1)]
    P = Prog(nc)
    C = {}
    C["W"] = declare_weights(nc, wprep)
    C["cst_dram"] = declare_consts(nc, consts)
    C["colbuf"] = colbuf
    C["xn"] = FM(P, "xn", D, BF16)
    C["g"] = FM(P, "g", D, BF16)
    C["s"] = FM(P, "s", D, BF16)
    C["xstage"] = FM(P, "xstage", 128, F32)
    C["xostage"] = FM(P, "xostage", 128, F32)
    C["wstage"] = FM(P, "wstage", 128, F32, tokens=1024)
    C["scr_sq"] = FM(P, "scr_sq", 256, BF16)
    C["scr_b1"] = FM(P, "scr_b1", 128, BF16)
    C["scr_b2"] = FM(P, "scr_b2", 128, BF16)
    C["row_a"] = FM(P, "row_a", 1, F32)
    C["row_b"] = C["xostage"]
    C["row_c"] = C["xstage"]

    C["ps_a"] = P.psum("ps_a", [128, 2048], F32)
    C["ps_a_lo"] = P.pb_alias(C["ps_a"].ap[:, 0:1024], "ps_a_lo")
    C["ps_a_hi"] = P.pb_alias(C["ps_a"].ap[:, 1024:2048], "ps_a_hi")
    psb = P.psum("ps_b", [128, 2048], F32)
    C["ps_b"] = psb
    C["ps1"] = P.pb_alias(psb.ap[:, 0:512], "ps1")
    C["ps2"] = P.pb_alias(psb.ap[:, 512:1024], "ps2")
    C["ps3"] = P.pb_alias(psb.ap[:, 1024:1536], "ps3")
    C["ps4"] = P.pb_alias(psb.ap[:, 1536:2048], "ps4")
    C["stage_consumer"] = {}
    C["xo_store"] = {}
    C["g_spill"] = nc.dram_tensor("g_spill", [D, T], BF16).ap()
    if 0 in layers:
        C["q_spill"] = nc.dram_tensor("q_spill", [16, 128, T], BF16).ap()
        C["v_spill"] = nc.dram_tensor("v_spill", [16, 128, 2048], BF16).ap()
        C["l0_scores"] = nc.dram_tensor("l0_scores", [8, T], F32).ap()
    if 1 in layers:
        C["xn_spill"] = nc.dram_tensor("xn_spill", [D, T], BF16).ap()
        if CONFIG.get("l1_probe"):
            C["l1_probe"] = True
            C["l1_sel_out"] = nc.dram_tensor("l1_sel_out", [10, 2], BF16,
                                             kind="ExternalOutput").ap()
        elif CONFIG.get("l1_sel_host"):
            C["l1_sel_in"] = nc.dram_tensor("l1_sel", [10, 2], BF16,
                                            kind="ExternalInput").ap()
        else:
            C["cc_in"] = nc.dram_tensor("cc_in", [10, 2], BF16).ap()
            C["cc_out"] = nc.dram_tensor(
                "cc_out", [CONFIG.get("n_cores", 8) * 10, 2], BF16).ap()
    C["dump_aps"] = {}
    C["nc_handle"] = nc
    for nm in CONFIG.get("dump", []):
        fm_rows = {"xn": D, "g": D, "s": D}.get(nm, D)
        C["dump_aps"][nm] = nc.dram_tensor("dump_" + nm, [fm_rows, T], F32,
                                           kind="ExternalOutput").ap()
    load_consts(P, C)
    srcs = [x_in] + inter
    dsts = inter + [x_out]
    if CONFIG.get("l1_probe"):
        emit_layernorm(P, C, None, C["xn"], src_dram=x_in)
        P.barrier()
        emit_l1_ffn(P, C, 1, C["xn"], C["g"])
        P.barrier()
        P.build()
        return nc
    for i, li in enumerate(layers):
        emit_layer(P, C, li, srcs[i], dsts[i])
    P.barrier()
    P.build()
    return nc


def kernel(**inputs):
    consts = make_consts()
    wprep = {}
    for li in range(4):
        prep_layer_weights(li, inputs, wprep)
    prep_l3_weights(inputs, wprep)
    prep_l0_weights(inputs, wprep)
    prep_l2f_weights(inputs, wprep)
    prep_l1f_weights(inputs, wprep)

    x = np.asarray(inputs["L0_attn_x"], np.float32)
    xT = [np.ascontiguousarray(x[b].T) for b in range(8)]

    cur = xT
    sel_arr = None
    for g in CONFIG["groups"]:
        gw = {k: v for k, v in wprep.items()
              if any(k.startswith(f"L{li}") for li in g)}
        if 1 in g and CONFIG.get("use_probe", True):
            CONFIG["l1_probe"] = True
            ncp = build_group([1], gw, consts)
            CONFIG["l1_probe"] = False
            in_maps = []
            for b in range(8):
                m = {"x_in": cur[b]}
                m.update(gw)
                m.update({"c_" + k: v for k, v in consts.items()})
                in_maps.append(m)
            resp = run_bass_kernel_spmd(ncp, in_maps, core_ids=list(range(8)))
            sel_arr = np.asarray(resp.results[0]["l1_sel_out"])
            CONFIG["l1_sel_host"] = True
        nc = build_group(g, gw, consts)
        in_maps = []
        for b in range(8):
            m = {"x_in": cur[b]}
            if 1 in g and sel_arr is not None:
                m["l1_sel"] = sel_arr
            m.update(gw)
            m.update({"c_" + k: v for k, v in consts.items()})
            in_maps.append(m)
        res = run_bass_kernel_spmd(nc, in_maps, core_ids=list(range(8)))
        cur = [np.asarray(res.results[b]["x_out"]) for b in range(8)]
        CONFIG["l1_sel_host"] = False
    out = np.stack([c.T for c in cur], axis=0)
    return np.ascontiguousarray(out, dtype=np.float32)
